# version 11
# baseline (speedup 1.0000x reference)
import contextlib
import numpy as np
import ml_dtypes
import concourse.bass as bass
import concourse.mybir as mybir
from concourse.bass_utils import run_bass_kernel_spmd

F32 = mybir.dt.float32
BF16 = mybir.dt.bfloat16
AF = mybir.ActivationFunctionType
ALU = mybir.AluOpType
AX = mybir.AxisListType

D = 2048
H = 16
FC = 16
SEQ = 2048
NCORES = 8
SPC = 2
TILE = 512
NT = 8
NE = 8
HC = 8
EPSF = float(np.finfo(np.float32).eps)
EPS_LN = 1e-5
SCALE_ATT = float(128) ** -0.5
BF = ml_dtypes.bfloat16

ENGS = ("PE", "DV", "AC", "GP", "SY")

EGROUPS = ([0, 1], [2, 3], [4, 5], [6, 7, 8])


class Em:

    def __init__(self):
        self.ops = {e: [] for e in ENGS}
        self.cnt = {}
        self.step = {}
        self.lastw = {}

    def sem(self, name, step):
        self.cnt[name] = 0
        self.step[name] = step

    def op(self, eng, fn, inc=None, waits=()):
        ws = []
        for s, v in waits:
            if s is None or v is None or v <= 0:
                continue
            k = (eng, s)
            if self.lastw.get(k, 0) < v:
                ws.append((s, v))
                self.lastw[k] = v
        n = None
        if inc is not None:
            self.cnt[inc] += self.step[inc]
            n = self.cnt[inc]
        self.ops[eng].append((tuple(ws), fn, inc, n))
        return n


def call(meth, *args, **kw):
    def fn(e, _m=meth, _a=args, _k=kw):
        return getattr(e, _m)(*_a, **_k)
    return fn


class _Stop(Exception):
    pass


def build_nc(n_tiles=NT, dump=(), stop_stage=None):
    nc = bass.Bass("TRN2", target_bir_lowering=False)

    xT = nc.dram_tensor("xT", [SPC, D, SEQ], F32, kind="ExternalInput").ap()
    wqkvT = nc.dram_tensor("wqkvT", [48, FC, 128, 128], F32, kind="ExternalInput").ap()
    wprojT = nc.dram_tensor("wprojT", [16, FC, 128, 128], F32, kind="ExternalInput").ap()
    w1T = nc.dram_tensor("w1T", [NE, HC, FC, 128, 128], F32, kind="ExternalInput").ap()
    w3T = nc.dram_tensor("w3T", [NE, HC, FC, 128, 128], F32, kind="ExternalInput").ap()
    w2T = nc.dram_tensor("w2T", [16, NE, HC, 128, 128], F32, kind="ExternalInput").ap()
    ws1T = nc.dram_tensor("ws1T", [HC, FC, 128, 128], F32, kind="ExternalInput").ap()
    ws3T = nc.dram_tensor("ws3T", [HC, FC, 128, 128], F32, kind="ExternalInput").ap()
    ws2T = nc.dram_tensor("ws2T", [16, HC, 128, 128], F32, kind="ExternalInput").ap()
    gT = nc.dram_tensor("gT", [FC, 128, NE], F32, kind="ExternalInput").ap()
    nwq_in = nc.dram_tensor("nwq", [128, FC], F32, kind="ExternalInput").ap()
    nwo_in = nc.dram_tensor("nwo", [128, FC], F32, kind="ExternalInput").ap()
    nwp_in = nc.dram_tensor("nwp", [128, FC], F32, kind="ExternalInput").ap()
    srow_in = nc.dram_tensor("srow", [1, H], F32, kind="ExternalInput").ap()
    cst_in = nc.dram_tensor("cst", [128, 512], F32, kind="ExternalInput").ap()
    sel8_in = nc.dram_tensor("sel8", [8, 1024], BF16, kind="ExternalInput").ap()
    idnb_in = nc.dram_tensor("idnb", [128, 128], BF16, kind="ExternalInput").ap()
    idnf_in = nc.dram_tensor("idnf", [128, 128], F32, kind="ExternalInput").ap()
    onesb_in = nc.dram_tensor("onesb", [128, 1], BF16, kind="ExternalInput").ap()
    onesf_in = nc.dram_tensor("onesf", [1, 128], F32, kind="ExternalInput").ap()

    yT = nc.dram_tensor("yT", [SPC, D, SEQ], F32, kind="ExternalOutput").ap()

    wqq_d = nc.dram_tensor("wqq_d", [48, FC, 128, 128], BF16).ap()
    wqp_d = nc.dram_tensor("wqp_d", [16, FC, 128, 128], BF16).ap()
    w1b_d = nc.dram_tensor("w1b_d", [NE, HC, FC, 128, 128], BF16).ap()
    w3b_d = nc.dram_tensor("w3b_d", [NE, HC, FC, 128, 128], BF16).ap()
    w2b_d = nc.dram_tensor("w2b_d", [16, NE, HC, 128, 128], BF16).ap()
    ws1b_d = nc.dram_tensor("ws1b_d", [HC, FC, 128, 128], BF16).ap()
    ws3b_d = nc.dram_tensor("ws3b_d", [HC, FC, 128, 128], BF16).ap()
    ws2b_d = nc.dram_tensor("ws2b_d", [16, HC, 128, 128], BF16).ap()

    em = Em()
    for s in ("cPE", "cDV", "cAC", "cPL"):
        em.sem(s, 1)
    for s in ("sW", "sW2", "sO", "sC", "sQ", "sQW", "sX", "sK", "sB"):
        em.sem(s, 16)

    ctx = contextlib.ExitStack()
    with ctx:
        sb = lambda n, s, d: ctx.enter_context(nc.sbuf_tensor(n, s, d))
        psum = lambda n, s, d: ctx.enter_context(nc.psum_tensor(n, s, d))

        x_sb = sb("x_sb", [128, FC * TILE], BF16)
        xn_sb = sb("xn_sb", [128, FC * TILE], BF16)
        q_sb = sb("q_sb", [128, H * TILE], BF16)
        k_sb = sb("k_sb", [128, H * TILE], BF16)
        v_sb = sb("v_sb", [128, H * TILE], BF16)
        o_sb = sb("o_sb", [128, H * TILE], BF16)
        with contextlib.ExitStack() as qctx:
            qsrc = [qctx.enter_context(nc.sbuf_tensor(f"qsrc{i}", [128, 2048], F32))
                    for i in range(3)]
            qge = qctx.enter_context(nc.sbuf_tensor("qge", [128, 2048], F32))
            cst_sb = qctx.enter_context(nc.sbuf_tensor("cst_sb", [128, 512], F32))
            tm_sb = qctx.enter_context(nc.sbuf_tensor("tm_sb", [128, 128], F32))
        rt_sb = sb("rt_sb", [128, 16 * TILE], BF16)
        A_sb = sb("A_sb", [128, 24 * TILE], BF16)
        Sb_sb = sb("Sb_sb", [128, H * 128], BF16)
        mask_sb = sb("mask_sb", [128, H * 128], BF16)
        lam_sb = sb("lam_sb", [128, H * 128], BF16)
        ring = [sb(f"ring{i}", [128, 2048], BF16) for i in range(4)]
        ktmg = sb("ktmg", [128, 1024], BF16)
        vtmg = sb("vtmg", [128, 1024], BF16)
        qlmg = sb("qlmg", [128, 1024], BF16)
        scmg = sb("scmg", [128, 1024], BF16)
        stage = [sb(f"stage{i}", [128, 512], F32) for i in range(2)]
        u_s = [sb(f"u{i}", [128, 512], BF16) for i in range(2)]
        g3_s = [sb(f"g3{i}", [128, 512], BF16) for i in range(2)]
        t5_s = [sb(f"t5{i}", [128, 512], BF16) for i in range(2)]
        sgb = [sb(f"sgb{i}", [128, 512], BF16) for i in range(2)]
        t1_sb = sb("t1_sb", [128, 512], F32)
        sq_s = [sb(f"sq{i}", [128, 512], BF16) for i in range(2)]
        bm_sb = sb("bm_sb", [128, 512], F32)
        rho_sb = sb("rho_sb", [128, 512], F32)
        rhoa_sb = sb("rhoa_sb", [128, 512], F32)
        st = sb("st", [1, 2048], F32)
        sig_fb = sb("sig_fb", [8, 512], BF16)
        sel8_sb = sb("sel8_sb", [8, 1024], BF16)
        idn_b = sb("idn_b", [128, 128], BF16)
        idn_f = sb("idn_f", [128, 128], F32)
        onesb_sb = sb("onesb_sb", [128, 1], BF16)
        onesf_sb = sb("onesf_sb", [1, 128], F32)
        gate_sb = sb("gate_sb", [128, FC * NE], BF16)
        nwq_c = sb("nwq_c", [128, FC], F32)
        nwo_c = sb("nwo_c", [128, FC], F32)
        nwp_c = sb("nwp_c", [128, FC], F32)
        nwpp_c = sb("nwpp_c", [128, FC], F32)
        s2l_c = sb("s2l_c", [128, FC], BF16)
        lamr_c = sb("lamr_c", [128, H], F32)
        lamT_c = sb("lamT_c", [128, H], F32)
        negs_c = sb("negs_c", [128, H], F32)
        la_c = sb("la_c", [128, H], F32)
        mc_c = sb("mc_c", [128, 8], F32)
        srow_sb = sb("srow_sb", [1, 16], F32)
        lg_sb = sb("lg_sb", [128, 32], F32)
        sc_sb = sb("sc_sb", [128, 32], F32)
        mx_sb = sb("mx_sb", [128, 32], F32)
        sgt_sb = sb("sgt_sb", [128, 32], F32)

        P = [psum(f"P{i}", [128, 512], F32) for i in range(7)]
        P7 = psum("P7", [128, 1024], BF16)

        sems = {}
        for name in list(em.cnt):
            sems[name] = ctx.enter_context(nc.semaphore(name))

        stv = lambda i, w=512: st[0:1, i * 512:i * 512 + w]

        def pe(fn, inc=None, waits=()):
            return em.op("PE", fn, inc, waits)

        def dv(fn, inc=None, waits=()):
            return em.op("DV", fn, inc, waits)

        def acm(fn, inc=None, waits=()):
            return em.op("AC", fn, inc, waits)

        def gp(fn, inc=None, waits=()):
            return em.op("GP", fn, inc, waits)

        def sy(fn, inc=None, waits=()):
            return em.op("SY", fn, inc, waits)

        ring_rel = [[], [], [], []]
        ring_n = [0]

        def ring_load(src_ap3, width=2048, extra=()):
            slot = ring_n[0] % 4
            ring_n[0] += 1
            dst = ring[slot][:, 0:width].rearrange("p (a b) -> p a b", b=128)
            n = sy(call("dma_start", dst, src_ap3), inc="sW",
                   waits=tuple(ring_rel[slot]) + tuple(extra))
            ring_rel[slot] = []
            return slot, ("sW", n)

        def ring_load_cast(srcf32_ap3, wb_ap3, width=2048):
            slot = ring_n[0] % 4
            ring_n[0] += 1
            dst = ring[slot][:, 0:width].rearrange("p (a b) -> p a b", b=128)
            n = gp(call("dma_start", dst, srcf32_ap3), inc="sK",
                   waits=tuple(ring_rel[slot]))
            nb = sy(call("dma_start", wb_ap3,
                         ring[slot][:, 0:width].rearrange(
                             "p (a b) -> p a b", b=128)), inc="sB",
                    waits=(("sK", n),))
            ring_rel[slot] = [("sB", nb)]
            return slot, ("sK", n)

        bank_ev = {}

        def bank_wait(i):
            return bank_ev.get(i, (None, 0))

        def ckpt(name):
            if stop_stage == name:
                raise _Stop()

        dv(call("memset", mc_c[:, :], 0.0))

        def cload(dst, src):
            return sy(call("dma_start", dst, src), inc="sC")

        cload(cst_sb[:, :], cst_in)
        cload(sel8_sb[:, :], sel8_in)
        cload(idn_b[:, :], idnb_in)
        cload(idn_f[:, :], idnf_in)
        cload(onesb_sb[:, :], onesb_in)
        cload(onesf_sb[:, :], onesf_in)
        cload(nwq_c[:, :], nwq_in)
        cload(nwo_c[:, :], nwo_in)
        cload(nwp_c[:, :], nwp_in)
        nC = cload(srow_sb[:, :], srow_in)

        def x_load(tile, wait_ev):
            seq, t0 = tile // 4, (tile % 4) * TILE
            src = xT[seq, :, t0:t0 + TILE].rearrange("(fc p) t -> p fc t", p=128)
            dst = x_sb[:, :].rearrange("p (fc t) -> p fc t", fc=FC)
            return gp(call("dma_start", dst, src), inc="sX",
                      waits=(("cDV", wait_ev),))

        x_load(0, 0)

        nPE = pe(call("matmul", P[5][:, 0:16], onesf_sb[0:1, :],
                      srow_sb[0:1, :], start=True, stop=True),
                 inc="cPE", waits=(("sC", nC),))
        nDV = dv(call("tensor_scalar", negs_c[:, :], P[5][:, 0:16], -1.0,
                      None, op0=ALU.mult), inc="cDV",
                 waits=(("cPE", nPE),))
        bank_ev[5] = ("cDV", nDV)
        dv(call("drain"))
        acm(call("activation", lamT_c[:, :], negs_c[:, :], AF.Exp,
                 scale=128.0), inc="cAC", waits=(("cDV", nDV),))
        nDV2 = dv(call("tensor_tensor", la_c[:, :], cst_sb[:, 384:400],
                       negs_c[:, :], op=ALU.mult), inc="cDV")
        dv(call("drain"))
        acm(call("activation", lamr_c[:, :], la_c[:, :], AF.Exp), inc="cAC",
            waits=(("cDV", nDV2),))

        ev_mA = 0
        ev_mB = 0
        for h in range(H):
            nA = dv(call("scalar_tensor_tensor", t1_sb[:, 0:128],
                         cst_sb[:, 0:128], negs_c[:, h:h + 1],
                         cst_sb[:, 128:256], op0=ALU.mult, op1=ALU.subtract),
                    inc="cDV", waits=(("cDV", nDV), ("cAC", ev_mA)))
            ev_mA = acm(call("activation", mask_sb[:, h * 128:(h + 1) * 128],
                             t1_sb[:, 0:128], AF.Exp), inc="cAC",
                        waits=(("cDV", nA),))
            nB = dv(call("tensor_scalar", tm_sb[:, 0:128],
                         cst_sb[:, 256:384], negs_c[:, h:h + 1], None,
                         op0=ALU.mult), inc="cDV", waits=(("cAC", ev_mB),))
            ev_mB = acm(call("activation", lam_sb[:, h * 128:(h + 1) * 128],
                             tm_sb[:, 0:128], AF.Exp), inc="cAC",
                        waits=(("cDV", nB),))
        ev_masks = max(ev_mA, ev_mB)

        dv(call("tensor_tensor", nwpp_c[:, :], nwo_c[:, :], nwp_c[:, :],
                op=ALU.mult), waits=(("sC", nC),))
        ev_s2l = dv(call("tensor_tensor", s2l_c[:, :], nwo_c[:, :],
                         nwo_c[:, :], op=ALU.mult), inc="cDV")
        ev_nwpp = ev_s2l

        qsrc_rel = [[], [], []]
        qge_rel = []

        def quant(n_oc, wsrc, wdst, mccol, nelem):
            last_pe = 0
            for oc in range(n_oc):
                slot = oc % 3
                src = wsrc[oc].rearrange("fc p o -> p fc o")
                qdst = qsrc[slot][:, :].rearrange("p (a b) -> p a b", b=128)
                nL = sy(call("dma_start", qdst, src), inc="sQ",
                        waits=tuple(qsrc_rel[slot]))
                qsrc_rel[slot] = []
                nA = acm(call("activation", ring[oc % 2][:, :],
                              qsrc[slot][:, :], AF.Abs), inc="cAC",
                         waits=(("sQ", nL),) + tuple(ring_rel[oc % 2]))
                ring_rel[oc % 2] = []
                qsrc_rel[slot].append(("cAC", nA))
                for b in range(4):
                    last_pe = pe(call("matmul", P[6][0:1, 0:512],
                                      onesb_sb[:, :],
                                      ring[oc % 2][:, b * 512:(b + 1) * 512],
                                      start=(oc == 0 and b == 0),
                                      stop=(oc == n_oc - 1 and b == 3),
                                      skip_group_check=True), inc="cPE",
                                 waits=(("cAC", nA), bank_wait(6)))
                ring_rel[oc % 2] = [("cPE", last_pe)]
            n1 = dv(call("tensor_copy", stv(0, 512), P[6][0:1, 0:512]),
                    inc="cDV", waits=(("cPE", last_pe),))
            bank_ev[6] = ("cDV", n1)
            dv(call("tensor_reduce", st[0:1, 1024:1025], stv(0, 512),
                    axis=AX.X, op=ALU.add))
            dv(call("drain"))
            dv(call("tensor_scalar", st[0:1, 1026:1027], st[0:1, 1024:1025],
                    1.0 / nelem, None, op0=ALU.mult))
            dv(call("drain"))
            dv(call("tensor_scalar", st[0:1, 1026:1027], st[0:1, 1026:1027],
                    1e-5, None, op0=ALU.max))
            dv(call("drain"))
            dv(call("tensor_scalar", st[0:1, 1028:1029], st[0:1, 1026:1027],
                    0.5, None, op0=ALU.mult))
            dv(call("drain"))
            dv(call("tensor_scalar", st[0:1, 1029:1030], st[0:1, 1028:1029],
                    -1.0, None, op0=ALU.mult))
            dv(call("drain"))
            n3 = dv(call("tensor_copy", st[0:1, 1030:1031],
                         st[0:1, 1026:1027]), inc="cDV")
            nP5 = pe(call("matmul", P[5][:, 0:3], onesf_sb[0:1, :],
                          st[0:1, 1028:1031], start=True, stop=True),
                     inc="cPE", waits=(("cDV", n3), bank_wait(5)))
            nmc = dv(call("tensor_copy", mc_c[:, mccol:mccol + 3],
                          P[5][:, 0:3]), inc="cDV", waits=(("cPE", nP5),))
            bank_ev[5] = ("cDV", nmc)
            dv(call("drain"))
            last_w = 0
            for oc in range(n_oc):
                slot = oc % 3
                src = wsrc[oc].rearrange("fc p o -> p fc o")
                qdst = qsrc[slot][:, :].rearrange("p (a b) -> p a b", b=128)
                nL = sy(call("dma_start", qdst, src), inc="sQ",
                        waits=tuple(qsrc_rel[slot]))
                qsrc_rel[slot] = []
                nG = dv(call("tensor_scalar", qge[:, :], qsrc[slot][:, :],
                             mc_c[:, mccol + 1:mccol + 2], None,
                             op0=ALU.is_le), inc="cDV",
                        waits=(("sQ", nL), ("cDV", nmc)) + tuple(qge_rel))
                qge_rel.clear()
                rslot = 2 + (oc % 2)
                nT = dv(call("scalar_tensor_tensor", ring[rslot][:, :],
                             qsrc[slot][:, :], mc_c[:, mccol:mccol + 1],
                             qge[:, :], op0=ALU.is_ge, op1=ALU.subtract),
                        inc="cDV", waits=tuple(ring_rel[rslot]))
                ring_rel[rslot] = []
                qsrc_rel[slot].append(("cDV", nT))
                qge_rel.append(("cDV", nT))
                dst = wdst[oc].rearrange("fc p o -> p fc o")
                last_w = sy(call(
                    "dma_start", dst,
                    ring[rslot][:, :].rearrange("p (a b) -> p a b", b=128)),
                    inc="sQW", waits=(("cDV", nT),))
                ring_rel[rslot] = [("sQW", last_w)]
            return last_w, nmc

        nQW_qkv, nmc_q = quant(48, wqkvT, wqq_d, 0, float(3 * D * D))
        nQW_proj, nmc_p = quant(16, wprojT, wqp_d, 4, float(D * D))
        nmc_done = max(nmc_q, nmc_p)

        kev_gate = gp(call("dma_start",
                           gate_sb[:, :].rearrange("p (a b) -> p a b", b=NE),
                           gT.rearrange("fc p e -> p fc e")), inc="sK")

        ev_x_free = 0
        ev_xn_prev_rd = 0
        ev_rt_read = 0
        ev_A_free = 0
        ev_stage = [0, 0]
        ev_u = [0, 0]
        ev_g3 = [0, 0]
        ev_t5 = [0, 0]
        ev_sq = [0, 0]
        ev_sig_rd = 0
        ev_sig = 0
        ev_sig_consumed = 0
        ev_P7 = 0
        ev_scm = [0, 0]
        ev_ktv = [0, 0]
        ev_qlm = [0, 0]
        ev_S = {}
        ev_o_evac = 0
        ev_z_prev = 0
        ev_qkv_war_PE = 0
        ev_qkv_war_PL = 0
        ev_qkv_evac = {}

        def ln_stats(gate_waits):
            last_st = 0
            for fc in range(FC):
                xsl = x_sb[:, fc * TILE:(fc + 1) * TILE]
                sqs = sq_s[fc % 2]
                nq = acm(call("activation", sqs[:, :], xsl, AF.Square),
                         inc="cAC",
                         waits=tuple(gate_waits) + (("cPE", ev_sq[fc % 2]),))
                pe(call("matmul", P[6][0:1, 0:512], onesb_sb[:, :], xsl,
                        start=(fc == 0), stop=(fc == FC - 1),
                        skip_group_check=True),
                   waits=tuple(gate_waits) + (bank_wait(6),))
                last_st = pe(call("matmul", P[5][0:1, 0:512], onesb_sb[:, :],
                                  sqs[:, :], start=(fc == 0),
                                  stop=(fc == FC - 1),
                                  skip_group_check=True), inc="cPE",
                             waits=(("cAC", nq), bank_wait(5)))
                ev_sq[fc % 2] = last_st
            return last_st

        def ln_rows(last_st, eps, fold_eps):
            n0 = dv(call("tensor_scalar", stv(0), P[6][0:1, 0:512], 1.0 / D,
                         None, op0=ALU.mult), inc="cDV",
                    waits=(("cPE", last_st),))
            bank_ev[6] = ("cDV", n0)
            n5 = dv(call("tensor_scalar", stv(1), P[5][0:1, 0:512], 1.0 / D,
                         None, op0=ALU.mult), inc="cDV")
            bank_ev[5] = ("cDV", n5)
            dv(call("tensor_tensor", stv(3), stv(0), stv(0), op=ALU.mult))
            dv(call("tensor_tensor", stv(1), stv(1), stv(3), op=ALU.subtract))
            nv = dv(call("tensor_scalar", stv(2), stv(1), eps, None,
                         op0=ALU.add), inc="cDV")
            acm(call("activation", stv(2), stv(2), AF.Ln), inc="cAC",
                waits=(("cDV", nv),))
            nr = acm(call("activation", stv(2), stv(2), AF.Exp, scale=-0.5),
                     inc="cAC")
            if fold_eps is None:
                return nr
            nf = dv(call("tensor_tensor", stv(3), stv(2), stv(2),
                         op=ALU.mult), inc="cDV", waits=(("cAC", nr),))
            dv(call("tensor_tensor", stv(3), stv(1), stv(3), op=ALU.mult))
            nf = dv(call("tensor_scalar", stv(3), stv(3), fold_eps, None,
                         op0=ALU.add), inc="cDV")
            acm(call("activation", stv(3), stv(3), AF.Ln), inc="cAC",
                waits=(("cDV", nf),))
            na = acm(call("activation", stv(3), stv(3), AF.Exp, scale=-0.5),
                     inc="cAC")
            nrho = dv(call("tensor_tensor", stv(2), stv(2), stv(3),
                           op=ALU.mult), inc="cDV", waits=(("cAC", na),))
            return nrho

        def bcast2(row_m, row_r, rows_done):
            nbm = pe(call("matmul", P[5][:, 0:512], onesf_sb[0:1, :], row_m,
                          start=True, stop=True), inc="cPE",
                     waits=(rows_done, bank_wait(5)))
            nbe = dv(call("tensor_copy", bm_sb[:, :], P[5][:, 0:512]),
                     inc="cDV", waits=(("cPE", nbm),))
            bank_ev[5] = ("cDV", nbe)
            nrh = pe(call("matmul", P[6][:, 0:512], onesf_sb[0:1, :], row_r,
                          start=True, stop=True), inc="cPE",
                     waits=(bank_wait(6),))
            nre = dv(call("tensor_copy", rho_sb[:, :], P[6][:, 0:512]),
                     inc="cDV", waits=(("cPE", nrh),))
            bank_ev[6] = ("cDV", nre)
            return nbe, nre

        try:
          for tile in range(n_tiles):
            seq, t0 = tile // 4, (tile % 4) * TILE
            if tile > 0:
                x_load(tile, ev_x_free)
            nXl = em.cnt["sX"]
            ckpt("xload")

            last_st = ln_stats([("sX", nXl)])
            nrho = ln_rows(last_st, EPS_LN, EPSF)
            nbe, nre = bcast2(stv(0), stv(2), ("cDV", nrho))
            ev_xn_fc = []
            for fc in range(FC):
                xsl = x_sb[:, fc * TILE:(fc + 1) * TILE]
                dv(call("tensor_tensor", t1_sb[:, :], xsl, bm_sb[:, :],
                        op=ALU.subtract),
                   waits=(("cDV", nbe), ("cPE", ev_xn_prev_rd),
                          ("cAC", ev_masks)))
                ev_xn_fc.append(dv(call("scalar_tensor_tensor",
                                        xn_sb[:, fc * TILE:(fc + 1) * TILE],
                                        t1_sb[:, :], nwq_c[:, fc:fc + 1],
                                        rho_sb[:, :], op0=ALU.mult,
                                        op1=ALU.mult),
                                   inc="cDV", waits=(("cDV", nre),)))
            ev_xn = ev_xn_fc[-1]
            ckpt("ln1")

            for oc in range(48):
                src = wqq_d[oc].rearrange("fc p o -> p fc o")
                extra = (("sQW", 16 * (oc + 1)),) if tile == 0 else ()
                slot, nW = ring_load(src, extra=extra)
                pacc = P[oc % 2]
                last = 0
                for fcK in range(FC):
                    last = pe(call("matmul", pacc[:, 0:512],
                                   ring[slot][:, fcK * 128:(fcK + 1) * 128],
                                   xn_sb[:, fcK * TILE:(fcK + 1) * TILE],
                                   start=(fcK == 0), stop=(fcK == FC - 1)),
                              inc=("cPE" if fcK == FC - 1 else None),
                              waits=(nW, ("cDV", ev_xn_fc[fcK]),
                                     bank_wait(oc % 2)))
                ring_rel[slot] = [("cPE", last)]
                if oc < 16:
                    dstb, o2 = q_sb, oc
                elif oc < 32:
                    dstb, o2 = k_sb, oc - 16
                else:
                    dstb, o2 = v_sb, oc - 32
                nE = acm(call("activation",
                              dstb[:, o2 * TILE:(o2 + 1) * TILE],
                              pacc[:, 0:512], AF.Copy, scale=mc_c[:, 2:3]),
                         inc="cAC",
                         waits=(("cPE", last), ("cDV", nmc_done),
                                ("cPE", ev_qkv_war_PE),
                                ("cPL", ev_qkv_war_PL)))
                bank_ev[oc % 2] = ("cAC", nE)
                ev_qkv_evac[oc] = nE
            ev_qkv_pe = em.cnt["cPE"]
            ckpt("qkv")

            for cc in range(4):
                gch = (tile % 4) * 4 + cc
                first = (gch == 0)
                tc = cc * 128
                for hg in range(4):
                    h0 = hg * 4
                    gslot = hg % 2
                    ntr = 0
                    for j in range(4):
                        h = h0 + j
                        pe(call("transpose", P7[:, j * 128:(j + 1) * 128],
                                k_sb[:, h * TILE + tc:h * TILE + tc + 128],
                                idn_b[:, :]),
                           waits=(("cAC", ev_qkv_evac[16 + h]),
                                  ("cAC", ev_P7)))
                    for j in range(4):
                        h = h0 + j
                        ntr = pe(call("transpose",
                                      P7[:, 512 + j * 128:512 + (j + 1) * 128],
                                      v_sb[:, h * TILE + tc:h * TILE + tc + 128],
                                      idn_b[:, :]),
                                 inc=("cPE" if j == 3 else None),
                                 waits=(("cAC", ev_qkv_evac[32 + h]),))
                    for j in range(4):
                        h = h0 + j
                        acm(call("activation",
                                 ktmg[:, gslot * 512 + j * 128:
                                      gslot * 512 + (j + 1) * 128],
                                 P7[:, j * 128:(j + 1) * 128], AF.Copy,
                                 scale=lamr_c[:, h:h + 1]),
                            waits=(("cPE", ntr), ("cPE", ev_ktv[gslot])))
                    nkv = 0
                    for j in range(4):
                        nkv = acm(call("activation",
                                       vtmg[:, gslot * 512 + j * 128:
                                            gslot * 512 + (j + 1) * 128],
                                       P7[:, 512 + j * 128:512 + (j + 1) * 128],
                                       AF.Copy),
                                  inc=("cAC" if j == 3 else None))
                    ev_P7 = nkv
                    nql = 0
                    for j in range(4):
                        h = h0 + j
                        nql = gp(call("tensor_tensor",
                                      qlmg[:, gslot * 512 + j * 128:
                                           gslot * 512 + (j + 1) * 128],
                                      q_sb[:, h * TILE + tc:h * TILE + tc + 128],
                                      lam_sb[:, h * 128:(h + 1) * 128],
                                      op=ALU.mult),
                                 inc=("cPL" if j == 3 else None),
                                 waits=(("cAC", ev_qkv_evac[h]),
                                        ("cAC", ev_masks),
                                        ("cPE", ev_qlm[gslot])))
                    nsc = 0
                    for j in range(4):
                        h = h0 + j
                        nsc = pe(call("matmul",
                                      P[2][:, j * 128:(j + 1) * 128],
                                      k_sb[:, h * TILE + tc:h * TILE + tc + 128],
                                      q_sb[:, h * TILE + tc:h * TILE + tc + 128],
                                      start=(j == 0), stop=(j == 3),
                                      skip_group_check=True),
                                 inc=("cPE" if j == 3 else None),
                                 waits=(("cAC", ev_qkv_evac[16 + h]),
                                        ("cAC", ev_qkv_evac[h]),
                                        bank_wait(2)))
                    nscm = 0
                    for j in range(4):
                        h = h0 + j
                        nscm = dv(call("tensor_tensor",
                                       scmg[:, gslot * 512 + j * 128:
                                            gslot * 512 + (j + 1) * 128],
                                       P[2][:, j * 128:(j + 1) * 128],
                                       mask_sb[:, h * 128:(h + 1) * 128],
                                       op=ALU.mult),
                                  inc=("cDV" if j == 3 else None),
                                  waits=(("cPE", nsc), ("cAC", ev_masks),
                                         ("cPE", ev_scm[gslot])))
                    bank_ev[2] = ("cDV", nscm)
                    no = 0
                    for j in range(4):
                        no = pe(call("matmul", P[3][:, j * 128:(j + 1) * 128],
                                     vtmg[:, gslot * 512 + j * 128:
                                          gslot * 512 + (j + 1) * 128],
                                     scmg[:, gslot * 512 + j * 128:
                                          gslot * 512 + (j + 1) * 128],
                                     start=(j == 0),
                                     stop=(first and j == 3),
                                     skip_group_check=True),
                                inc=("cPE" if (first and j == 3) else None),
                                waits=(("cAC", nkv), ("cDV", nscm),
                                       ("cAC", ev_o_evac)))
                    if not first:
                        for j in range(4):
                            h = h0 + j
                            no = pe(call("matmul",
                                         P[3][:, j * 128:(j + 1) * 128],
                                         Sb_sb[:, h * 128:(h + 1) * 128],
                                         qlmg[:, gslot * 512 + j * 128:
                                              gslot * 512 + (j + 1) * 128],
                                         start=False, stop=(j == 3),
                                         skip_group_check=True),
                                    inc=("cPE" if j == 3 else None),
                                    waits=(("cPL", nql),
                                           ("cDV", ev_S.get((gch - 1, hg),
                                                            0))))
                    ev_scm[gslot] = no
                    ev_qlm[gslot] = no
                    ov = o_sb[:, :].rearrange("p (h t) -> p h t", h=H)
                    ev_o_evac = acm(call("activation",
                                         ov[:, h0:h0 + 4, tc:tc + 128],
                                         P[3][:, 0:512].rearrange(
                                             "p (a b) -> p a b", a=4),
                                         AF.Copy, scale=SCALE_ATT),
                                    inc="cAC",
                                    waits=(("cPE", no), ("cDV", ev_z_prev)))
                    bank_ev[3] = ("cAC", ev_o_evac)
                    nkvm = 0
                    for j in range(4):
                        nkvm = pe(call("matmul",
                                       P[4][:, j * 128:(j + 1) * 128],
                                       ktmg[:, gslot * 512 + j * 128:
                                            gslot * 512 + (j + 1) * 128],
                                       vtmg[:, gslot * 512 + j * 128:
                                            gslot * 512 + (j + 1) * 128],
                                       start=(j == 0), stop=(j == 3),
                                       skip_group_check=True),
                                  inc=("cPE" if j == 3 else None),
                                  waits=(("cAC", nkv), bank_wait(4)))
                    ev_ktv[gslot] = nkvm
                    nS = 0
                    for j in range(4):
                        h = h0 + j
                        if first:
                            nS = dv(call("tensor_copy",
                                         Sb_sb[:, h * 128:(h + 1) * 128],
                                         P[4][:, j * 128:(j + 1) * 128]),
                                    inc=("cDV" if j == 3 else None),
                                    waits=(("cPE", nkvm),))
                        else:
                            nS = dv(call("scalar_tensor_tensor",
                                         Sb_sb[:, h * 128:(h + 1) * 128],
                                         Sb_sb[:, h * 128:(h + 1) * 128],
                                         lamT_c[:, h:h + 1],
                                         P[4][:, j * 128:(j + 1) * 128],
                                         op0=ALU.mult, op1=ALU.add),
                                    inc=("cDV" if j == 3 else None),
                                    waits=(("cPE", nkvm),))
                    bank_ev[4] = ("cDV", nS)
                    ev_S[(gch, hg)] = nS
            ev_qkv_war_PE = em.cnt["cPE"]
            ev_qkv_war_PL = em.cnt["cPL"]
            ckpt("attn")

            last_st = 0
            for fc in range(FC):
                osl = o_sb[:, fc * TILE:(fc + 1) * TILE]
                sqs = sq_s[fc % 2]
                nq = acm(call("activation", sqs[:, :], osl, AF.Square),
                         inc="cAC",
                         waits=(("cAC", ev_o_evac), ("cPE", ev_sq[fc % 2])))
                pe(call("matmul", P[6][0:1, 0:512], onesb_sb[:, :],
                        sqs[:, :], start=(fc == 0), stop=(fc == FC - 1),
                        skip_group_check=True),
                   waits=(("cAC", nq), bank_wait(6)))
                last_st = pe(call("matmul", P[5][0:1, 0:512],
                                  s2l_c[:, fc:fc + 1], sqs[:, :],
                                  start=(fc == 0), stop=(fc == FC - 1),
                                  skip_group_check=True), inc="cPE",
                             waits=(("cDV", ev_s2l), bank_wait(5)))
                ev_sq[fc % 2] = last_st
            n0 = dv(call("tensor_scalar", stv(0), P[6][0:1, 0:512], 1.0 / D,
                         None, op0=ALU.mult), inc="cDV",
                    waits=(("cPE", last_st),))
            bank_ev[6] = ("cDV", n0)
            n5 = dv(call("tensor_scalar", stv(1), P[5][0:1, 0:512], 1.0 / D,
                         None, op0=ALU.mult), inc="cDV")
            bank_ev[5] = ("cDV", n5)
            nv = dv(call("tensor_scalar", stv(2), stv(0), 1e-6, None,
                         op0=ALU.add), inc="cDV")
            acm(call("activation", stv(2), stv(2), AF.Ln), inc="cAC",
                waits=(("cDV", nv),))
            nr = acm(call("activation", stv(2), stv(2), AF.Exp, scale=-0.5),
                     inc="cAC")
            nf = dv(call("tensor_tensor", stv(3), stv(2), stv(2),
                         op=ALU.mult), inc="cDV", waits=(("cAC", nr),))
            dv(call("tensor_tensor", stv(3), stv(1), stv(3), op=ALU.mult))
            nf = dv(call("tensor_scalar", stv(3), stv(3), EPSF, None,
                         op0=ALU.add), inc="cDV")
            acm(call("activation", stv(3), stv(3), AF.Ln), inc="cAC",
                waits=(("cDV", nf),))
            na = acm(call("activation", stv(3), stv(3), AF.Exp, scale=-0.5),
                     inc="cAC")
            nf = dv(call("tensor_tensor", stv(2), stv(2), stv(3),
                         op=ALU.mult), inc="cDV", waits=(("cAC", na),))
            nrh = pe(call("matmul", P[5][:, 0:512], onesf_sb[0:1, :], stv(2),
                          start=True, stop=True), inc="cPE",
                     waits=(("cDV", nf), bank_wait(5)))
            nra = dv(call("tensor_copy", rhoa_sb[:, :], P[5][:, 0:512]),
                     inc="cDV", waits=(("cPE", nrh),))
            bank_ev[5] = ("cDV", nra)

            ev_z_fc = []
            for fc in range(FC):
                ev_z_fc.append(dv(call("scalar_tensor_tensor",
                                       xn_sb[:, fc * TILE:(fc + 1) * TILE],
                                       o_sb[:, fc * TILE:(fc + 1) * TILE],
                                       nwpp_c[:, fc:fc + 1], rhoa_sb[:, :],
                                       op0=ALU.mult, op1=ALU.mult),
                                  inc="cDV",
                                  waits=(("cDV", nra), ("cPE", ev_qkv_pe),
                                         ("cDV", ev_nwpp))))
            ev_z = ev_z_fc[-1]
            ev_z_prev = ev_z

            for oc in range(16):
                src = wqp_d[oc].rearrange("fc p o -> p fc o")
                extra = ((("sQW", nQW_qkv + 16 * (oc + 1)),)
                         if tile == 0 else ())
                slot, nW = ring_load(src, extra=extra)
                pacc = P[oc % 2]
                last = 0
                for fcK in range(FC):
                    last = pe(call("matmul", pacc[:, 0:512],
                                   ring[slot][:, fcK * 128:(fcK + 1) * 128],
                                   xn_sb[:, fcK * TILE:(fcK + 1) * TILE],
                                   start=(fcK == 0), stop=(fcK == FC - 1)),
                              inc=("cPE" if fcK == FC - 1 else None),
                              waits=(nW, ("cDV", ev_z_fc[fcK]),
                                     bank_wait(oc % 2)))
                ring_rel[slot] = [("cPE", last)]
                nE = dv(call("scalar_tensor_tensor",
                             x_sb[:, oc * TILE:(oc + 1) * TILE],
                             pacc[:, 0:512], mc_c[:, 6:7],
                             x_sb[:, oc * TILE:(oc + 1) * TILE],
                             op0=ALU.mult, op1=ALU.add), inc="cDV",
                        waits=(("cPE", last), ("cDV", nmc_done)))
                bank_ev[oc % 2] = ("cDV", nE)
            ev_proj_pe = em.cnt["cPE"]
            ev_x2 = em.cnt["cDV"]
            ckpt("proj")

            last_st = ln_stats([("cDV", ev_x2)])
            nr2 = ln_rows(last_st, EPS_LN, None)
            nbe, nre = bcast2(stv(0), stv(2), ("cAC", nr2))
            ev_xn2_fc = []
            for fc in range(FC):
                xsl = x_sb[:, fc * TILE:(fc + 1) * TILE]
                dv(call("tensor_tensor", t1_sb[:, :], xsl, bm_sb[:, :],
                        op=ALU.subtract), waits=(("cDV", nbe),))
                ev_xn2_fc.append(dv(call("tensor_tensor",
                                         xn_sb[:, fc * TILE:(fc + 1) * TILE],
                                         t1_sb[:, :], rho_sb[:, :],
                                         op=ALU.mult), inc="cDV",
                                    waits=(("cDV", nre),
                                           ("cPE", ev_proj_pe))))
            ev_xn_full = ev_xn2_fc[-1]
            ckpt("ln2")

            ngl = 0
            for tt in range(4):
                for fcK in range(FC):
                    ngl = pe(call("matmul", P[6][:, tt * 8:(tt + 1) * 8],
                                  xn_sb[:, fcK * TILE + tt * 128:
                                        fcK * TILE + tt * 128 + 128],
                                  gate_sb[:, fcK * 8:(fcK + 1) * 8],
                                  start=(tt == 0 and fcK == 0),
                                  stop=(fcK == FC - 1),
                                  skip_group_check=True),
                             inc=("cPE" if (fcK == FC - 1 and tt == 3)
                                  else None),
                             waits=(("cDV", ev_xn2_fc[fcK]),
                                    ("sK", kev_gate), bank_wait(6)))
            nlg = dv(call("tensor_copy", lg_sb[:, :], P[6][:, 0:32]),
                     inc="cDV", waits=(("cPE", ngl),))
            bank_ev[6] = ("cDV", nlg)
            nsg = acm(call("activation", sc_sb[:, :], lg_sb[:, :],
                           AF.Sigmoid), inc="cAC", waits=(("cDV", nlg),))
            for tt in range(4):
                dv(call("max", out=mx_sb[:, tt * 8:(tt + 1) * 8],
                        in_=sc_sb[:, tt * 8:(tt + 1) * 8]),
                   waits=(("cAC", nsg),))
            dv(call("drain"))
            nsig = 0
            for tt in range(4):
                nsig = dv(call("scalar_tensor_tensor",
                               sgt_sb[:, tt * 8:(tt + 1) * 8],
                               sc_sb[:, tt * 8:(tt + 1) * 8],
                               mx_sb[:, tt * 8 + 1:tt * 8 + 2],
                               sc_sb[:, tt * 8:(tt + 1) * 8],
                               op0=ALU.is_ge, op1=ALU.mult),
                          inc=("cDV" if tt == 3 else None),
                          waits=(("cPE", ev_sig_rd),))
            dv(call("drain"))
            ntr = 0
            for tt in range(4):
                ntr = pe(call("transpose", P[5][0:8, tt * 128:(tt + 1) * 128],
                              sgt_sb[:, tt * 8:(tt + 1) * 8], idn_f[:, :]),
                         inc=("cPE" if tt == 3 else None),
                         waits=(("cDV", nsig), bank_wait(5)))
            ev_sig_rd = ntr
            ev_sig = dv(call("tensor_copy", sig_fb[:, :], P[5][0:8, 0:512]),
                        inc="cDV", waits=(("cPE", ntr),
                                          ("cPE", ev_sig_consumed)))
            bank_ev[5] = ("cDV", ev_sig)
            ckpt("gate")

            for gi, exps in enumerate(EGROUPS):
                nsb = 0
                for e_ in exps:
                    shared = (e_ == 8)
                    sslot = e_ % 2
                    if not shared:
                        nbc = pe(call("matmul", P[5][:, 0:512],
                                      sel8_sb[0:8, e_ * 128:(e_ + 1) * 128],
                                      sig_fb[0:8, :], start=True, stop=True),
                                 inc="cPE",
                                 waits=(("cDV", ev_sig), bank_wait(5)))
                        nsb = dv(call("tensor_copy", sgb[sslot][:, :],
                                      P[5][:, 0:512]), inc="cDV",
                                 waits=(("cPE", nbc),))
                        bank_ev[5] = ("cDV", nsb)
                    for hc in range(HC):
                        aslot = (e_ % 2) * 8 + hc if e_ < 8 else 16 + hc
                        uslot = hc % 2
                        bk1, bk3 = (2 * hc) % 2, (2 * hc + 1) % 2
                        if tile == 0:
                            srcf = (w1T[e_, hc] if not shared else ws1T[hc]
                                    ).rearrange("fc p h -> p fc h")
                            wb = (w1b_d[e_, hc] if not shared else ws1b_d[hc]
                                  ).rearrange("fc p h -> p fc h")
                            slot, nW = ring_load_cast(srcf, wb)
                        else:
                            src = (w1b_d[e_, hc] if not shared else
                                   ws1b_d[hc]).rearrange("fc p h -> p fc h")
                            slot, nW = ring_load(
                                src, extra=(("sB", nB_all),))
                        pacc = P[bk1]
                        last = 0
                        for fcK in range(FC):
                            last = pe(call(
                                "matmul", pacc[:, 0:512],
                                ring[slot][:, fcK * 128:(fcK + 1) * 128],
                                xn_sb[:, fcK * TILE:(fcK + 1) * TILE],
                                start=(fcK == 0), stop=(fcK == FC - 1)),
                                inc=("cPE" if fcK == FC - 1 else None),
                                waits=(nW, ("cDV", ev_xn2_fc[fcK]),
                                       bank_wait(bk1)))
                        ring_rel[slot] = ring_rel[slot] + [("cPE", last)]
                        if shared:
                            nu = acm(call("activation", u_s[uslot][:, :],
                                          pacc[:, 0:512], AF.Silu),
                                     inc="cAC",
                                     waits=(("cPE", last),
                                            ("cPL", ev_u[uslot])))
                            bank_ev[bk1] = ("cAC", nu)
                        else:
                            nu0 = dv(call("tensor_tensor", u_s[uslot][:, :],
                                          pacc[:, 0:512], sgb[sslot][:, :],
                                          op=ALU.mult), inc="cDV",
                                     waits=(("cPE", last), ("cDV", nsb),
                                            ("cPL", ev_u[uslot])))
                            nu = acm(call("activation", u_s[uslot][:, :],
                                          u_s[uslot][:, :], AF.Silu),
                                     inc="cAC", waits=(("cDV", nu0),))
                            bank_ev[bk1] = ("cDV", nu0)
                        if tile == 0:
                            srcf = (w3T[e_, hc] if not shared else ws3T[hc]
                                    ).rearrange("fc p h -> p fc h")
                            wb = (w3b_d[e_, hc] if not shared else ws3b_d[hc]
                                  ).rearrange("fc p h -> p fc h")
                            slot, nW = ring_load_cast(srcf, wb)
                        else:
                            src = (w3b_d[e_, hc] if not shared else
                                   ws3b_d[hc]).rearrange("fc p h -> p fc h")
                            slot, nW = ring_load(src)
                        pacc = P[bk3]
                        last = 0
                        for fcK in range(FC):
                            last = pe(call(
                                "matmul", pacc[:, 0:512],
                                ring[slot][:, fcK * 128:(fcK + 1) * 128],
                                xn_sb[:, fcK * TILE:(fcK + 1) * TILE],
                                start=(fcK == 0), stop=(fcK == FC - 1)),
                                inc=("cPE" if fcK == FC - 1 else None),
                                waits=(nW, bank_wait(bk3)))
                        ring_rel[slot] = ring_rel[slot] + [("cPE", last)]
                        ng3 = acm(call("activation", g3_s[uslot][:, :],
                                       pacc[:, 0:512], AF.Copy), inc="cAC",
                                  waits=(("cPE", last),
                                         ("cPL", ev_g3[uslot])))
                        bank_ev[bk3] = ("cAC", ng3)
                        if shared:
                            nA = gp(call("tensor_tensor",
                                         A_sb[:, aslot * TILE:
                                              (aslot + 1) * TILE],
                                         u_s[uslot][:, :], g3_s[uslot][:, :],
                                         op=ALU.mult), inc="cPL",
                                    waits=(("cAC", nu), ("cAC", ng3),
                                           ("cPE", ev_A_free)))
                            ev_u[uslot] = nA
                            ev_g3[uslot] = nA
                        else:
                            nt5 = gp(call("tensor_tensor", t5_s[uslot][:, :],
                                          u_s[uslot][:, :], g3_s[uslot][:, :],
                                          op=ALU.mult), inc="cPL",
                                     waits=(("cAC", nu), ("cAC", ng3),
                                            ("cDV", ev_t5[uslot])))
                            ev_u[uslot] = nt5
                            ev_g3[uslot] = nt5
                            nA = dv(call("tensor_tensor",
                                         A_sb[:, aslot * TILE:
                                              (aslot + 1) * TILE],
                                         t5_s[uslot][:, :], sgb[sslot][:, :],
                                         op=ALU.mult), inc="cDV",
                                    waits=(("cPL", nt5), ("cPE", ev_A_free)))
                            ev_t5[uslot] = nA
                ev_G_done_DV = em.cnt["cDV"]
                ev_G_done_PL = em.cnt["cPL"]
                ev_sig_consumed = em.cnt["cPE"]
                ckpt("moeg")
                for oc in range(16):
                    pacc = P[oc % 2]
                    nmm = 0
                    for ei, e_ in enumerate(exps):
                        shared = (e_ == 8)
                        if tile == 0:
                            srcf = (w2T[oc, e_] if not shared else ws2T[oc]
                                    ).rearrange("hc p o -> p hc o")
                            wb = (w2b_d[oc, e_] if not shared else ws2b_d[oc]
                                  ).rearrange("hc p o -> p hc o")
                            slot, nW = ring_load_cast(srcf, wb, width=1024)
                        else:
                            src = (w2b_d[oc, e_] if not shared else
                                   ws2b_d[oc]).rearrange("hc p o -> p hc o")
                            slot, nW = ring_load(src, width=1024)
                        for hcK in range(HC):
                            aslot = (e_ % 2) * 8 + hcK if e_ < 8 else 16 + hcK
                            nmm = pe(call(
                                "matmul", pacc[:, 0:512],
                                ring[slot][:, hcK * 128:(hcK + 1) * 128],
                                A_sb[:, aslot * TILE:(aslot + 1) * TILE],
                                start=(ei == 0 and hcK == 0),
                                stop=(ei == len(exps) - 1 and hcK == HC - 1)),
                                inc=("cPE" if hcK == HC - 1 else None),
                                waits=(nW, ("cDV", ev_G_done_DV),
                                       ("cPL", ev_G_done_PL),
                                       bank_wait(oc % 2)))
                        ring_rel[slot] = ring_rel[slot] + [("cPE", nmm)]
                    if gi == 0:
                        nE = acm(call("activation",
                                      rt_sb[:, oc * TILE:(oc + 1) * TILE],
                                      pacc[:, 0:512], AF.Copy), inc="cAC",
                                 waits=(("cPE", nmm), ("cDV", ev_rt_read)))
                        bank_ev[oc % 2] = ("cAC", nE)
                    elif gi < 3:
                        nE = dv(call("tensor_tensor",
                                     rt_sb[:, oc * TILE:(oc + 1) * TILE],
                                     rt_sb[:, oc * TILE:(oc + 1) * TILE],
                                     pacc[:, 0:512], op=ALU.add), inc="cDV",
                                waits=(("cPE", nmm),))
                        bank_ev[oc % 2] = ("cDV", nE)
                    else:
                        sslot = oc % 2
                        n1 = dv(call("scalar_tensor_tensor",
                                     stage[sslot][:, :],
                                     rt_sb[:, oc * TILE:(oc + 1) * TILE],
                                     1.0, pacc[:, 0:512], op0=ALU.mult,
                                     op1=ALU.add), inc="cDV",
                                waits=(("cPE", nmm),
                                       ("sO", ev_stage[sslot])))
                        n2 = dv(call("tensor_tensor", stage[sslot][:, :],
                                     stage[sslot][:, :],
                                     x_sb[:, oc * TILE:(oc + 1) * TILE],
                                     op=ALU.add), inc="cDV")
                        bank_ev[oc % 2] = ("cDV", n1)
                        nO = sy(call("dma_start",
                                     yT[seq, oc * 128:(oc + 1) * 128,
                                        t0:t0 + TILE], stage[sslot][:, :]),
                                inc="sO", waits=(("cDV", n2),))
                        ev_stage[sslot] = nO
                ev_A_free = em.cnt["cPE"]
                if gi == 3:
                    ev_rt_read = em.cnt["cDV"]
            ev_x_free = em.cnt["cDV"]
            ev_xn_prev_rd = em.cnt["cPE"]
            if tile == 0:
                nB_all = em.cnt["sB"]
        except _Stop:
            pass

        if dump:
            all_bufs = dict(x_sb=x_sb, xn_sb=xn_sb, q_sb=q_sb, k_sb=k_sb,
                            v_sb=v_sb, o_sb=o_sb, A_sb=A_sb, rt_sb=rt_sb,
                            S_sb=Sb_sb, mask_sb=mask_sb, lam_sb=lam_sb,
                            sig_fb=sig_fb, mc_c=mc_c, sgt_sb=sgt_sb,
                            sc_sb=sc_sb, st=st, bm_sb=bm_sb, rho_sb=rho_sb,
                            rhoa_sb=rhoa_sb, lamr_c=lamr_c, lamT_c=lamT_c,
                            stage0=stage[0], sgb0=sgb[0], lg_sb=lg_sb,
                            mx_sb=mx_sb)
            fin = (("cDV", em.cnt["cDV"]), ("cAC", em.cnt["cAC"]),
                   ("cPE", em.cnt["cPE"]), ("cPL", em.cnt["cPL"]))
            for name in dump:
                t = all_bufs[name]
                shp = list(t.shape)
                dt = BF16 if t.dtype == BF16 else F32
                dten = nc.dram_tensor("dbg_" + name, shp, dt,
                                      kind="ExternalOutput").ap()
                sy(call("dma_start", dten, t[:, :]), inc="sO", waits=fin)

        sy(call("wait_ge", sems["sO"], em.cnt["sO"]))

        block = ctx.enter_context(nc.Block())
        eng_map = {"PE": block.tensor, "DV": block.vector, "AC": block.scalar,
                   "GP": block.gpsimd, "SY": block.sync}

        def replay(ename):
            def body(e):
                for ws, fn, inc, n in em.ops[ename]:
                    for s, v in ws:
                        e.wait_ge(sems[s], v)
                    ins = fn(e)
                    if inc is not None and ins is not None:
                        ins.then_inc(sems[inc], em.step[inc])
            return body

        for ename, dec in eng_map.items():
            dec(replay(ename))

    return nc



def _prep_shared(inputs):
    f32 = lambda a: np.ascontiguousarray(np.asarray(a, dtype=np.float32))
    qkv_w = f32(inputs["L0_attn_qkv_w"])
    proj_w = f32(inputs["L0_attn_proj_w"])
    w1 = f32(inputs["L0_ffn_w1"]); w2 = f32(inputs["L0_ffn_w2"])
    w3 = f32(inputs["L0_ffn_w3"])
    ws1 = f32(inputs["L0_ffn_ws1"]); ws2 = f32(inputs["L0_ffn_ws2"])
    ws3 = f32(inputs["L0_ffn_ws3"])
    gate_w = f32(inputs["L0_ffn_gate_w"])

    d = {}
    d["wqkvT"] = np.ascontiguousarray(
        qkv_w.reshape(48, 128, FC, 128).transpose(0, 2, 3, 1))
    d["wprojT"] = np.ascontiguousarray(
        proj_w.reshape(16, 128, FC, 128).transpose(0, 2, 3, 1))
    d["w1T"] = np.ascontiguousarray(
        w1.reshape(NE, HC, 128, FC, 128).transpose(0, 1, 3, 4, 2))
    d["w3T"] = np.ascontiguousarray(
        w3.reshape(NE, HC, 128, FC, 128).transpose(0, 1, 3, 4, 2))
    d["w2T"] = np.ascontiguousarray(
        w2.reshape(NE, 16, 128, HC, 128).transpose(1, 0, 3, 4, 2))
    d["ws1T"] = np.ascontiguousarray(
        ws1.reshape(HC, 128, FC, 128).transpose(0, 2, 3, 1))
    d["ws3T"] = np.ascontiguousarray(
        ws3.reshape(HC, 128, FC, 128).transpose(0, 2, 3, 1))
    d["ws2T"] = np.ascontiguousarray(
        ws2.reshape(16, 128, HC, 128).transpose(0, 2, 3, 1))
    d["gT"] = np.ascontiguousarray(gate_w.T.reshape(FC, 128, NE))
    d["nwq"] = np.ascontiguousarray(
        f32(inputs["L0_attn_qkv_norm_w"]).reshape(FC, 128).T)
    d["nwo"] = np.ascontiguousarray(
        f32(inputs["L0_attn_norm_w"]).reshape(FC, 128).T)
    d["nwp"] = np.ascontiguousarray(
        f32(inputs["L0_attn_proj_norm_w"]).reshape(FC, 128).T)
    d["srow"] = f32(inputs["L0_attn_s"]).reshape(1, H)

    p = np.arange(128, dtype=np.float32)
    cst = np.zeros((128, 512), dtype=np.float32)
    cst[:, 0:128] = p[None, :] - p[:, None]
    cst[:, 128:256] = 1e30 * (p[None, :] < p[:, None])
    cst[:, 256:384] = np.broadcast_to(p[None, :], (128, 128))
    cst[:, 384:400] = np.broadcast_to((128.0 - p)[:, None], (128, H))
    d["cst"] = cst
    sel8 = np.zeros((8, 1024), dtype=np.float32)
    for e in range(8):
        sel8[e, e * 128:(e + 1) * 128] = 1.0
    d["sel8"] = sel8.astype(BF)
    d["idnb"] = np.eye(128, dtype=np.float32).astype(BF)
    d["idnf"] = np.eye(128, dtype=np.float32)
    d["onesb"] = np.ones((128, 1), dtype=np.float32).astype(BF)
    d["onesf"] = np.ones((1, 128), dtype=np.float32)
    return d


def _prep_x(inputs):
    x = np.asarray(inputs["L0_attn_x"], dtype=np.float32)
    return [np.ascontiguousarray(x[SPC * c:SPC * (c + 1)].transpose(0, 2, 1))
            for c in range(NCORES)]


_BUILT = [None]


def kernel(**inputs):
    shared = _prep_shared(inputs)
    xs = _prep_x(inputs)
    if _BUILT[0] is None:
        _BUILT[0] = build_nc()
    nc = _BUILT[0]
    in_maps = [dict(shared, xT=xs[c]) for c in range(NCORES)]
    del shared, xs
    res = run_bass_kernel_spmd(nc, in_maps, core_ids=list(range(NCORES)))
    del in_maps
    out = np.empty((16, SEQ, D), dtype=np.float32)
    for c in range(NCORES):
        yTc = np.asarray(res.results[c]["yT"], dtype=np.float32)
        out[SPC * c:SPC * (c + 1)] = yTc.transpose(0, 2, 1)
    return out


# version 12 (final)
# speedup vs baseline: 1.0534x; 1.0534x over previous
import contextlib
import numpy as np
import ml_dtypes
import concourse.bass as bass
import concourse.mybir as mybir
from concourse.bass_utils import run_bass_kernel_spmd

F32 = mybir.dt.float32
BF16 = mybir.dt.bfloat16
AF = mybir.ActivationFunctionType
ALU = mybir.AluOpType
AX = mybir.AxisListType

D = 2048
H = 16
FC = 16
SEQ = 2048
NCORES = 8
SPC = 2
TILE = 512
NT = 8
NE = 8
HC = 8
EPSF = float(np.finfo(np.float32).eps)
EPS_LN = 1e-5
SCALE_ATT = float(128) ** -0.5
BF = ml_dtypes.bfloat16

ENGS = ("PE", "DV", "AC", "GP", "SY")

EGROUPS = ([0, 1], [2, 3], [4, 5], [6, 7, 8])


class Em:

    def __init__(self):
        self.ops = {e: [] for e in ENGS}
        self.cnt = {}
        self.step = {}
        self.lastw = {}

    def sem(self, name, step):
        self.cnt[name] = 0
        self.step[name] = step

    def op(self, eng, fn, inc=None, waits=()):
        ws = []
        for s, v in waits:
            if s is None or v is None or v <= 0:
                continue
            k = (eng, s)
            if self.lastw.get(k, 0) < v:
                ws.append((s, v))
                self.lastw[k] = v
        n = None
        if inc is not None:
            self.cnt[inc] += self.step[inc]
            n = self.cnt[inc]
        self.ops[eng].append((tuple(ws), fn, inc, n))
        return n


def call(meth, *args, **kw):
    def fn(e, _m=meth, _a=args, _k=kw):
        return getattr(e, _m)(*_a, **_k)
    return fn


class _Stop(Exception):
    pass


def build_nc(n_tiles=NT, dump=(), stop_stage=None):
    nc = bass.Bass("TRN2", target_bir_lowering=False)

    xT = nc.dram_tensor("xT", [SPC, D, SEQ], F32, kind="ExternalInput").ap()
    wqkvT = nc.dram_tensor("wqkvT", [48, FC, 128, 128], F32, kind="ExternalInput").ap()
    wprojT = nc.dram_tensor("wprojT", [16, FC, 128, 128], F32, kind="ExternalInput").ap()
    w1T = nc.dram_tensor("w1T", [NE, HC, FC, 128, 128], F32, kind="ExternalInput").ap()
    w3T = nc.dram_tensor("w3T", [NE, HC, FC, 128, 128], F32, kind="ExternalInput").ap()
    w2T = nc.dram_tensor("w2T", [16, NE, HC, 128, 128], F32, kind="ExternalInput").ap()
    ws1T = nc.dram_tensor("ws1T", [HC, FC, 128, 128], F32, kind="ExternalInput").ap()
    ws3T = nc.dram_tensor("ws3T", [HC, FC, 128, 128], F32, kind="ExternalInput").ap()
    ws2T = nc.dram_tensor("ws2T", [16, HC, 128, 128], F32, kind="ExternalInput").ap()
    gT = nc.dram_tensor("gT", [FC, 128, NE], F32, kind="ExternalInput").ap()
    nwq_in = nc.dram_tensor("nwq", [128, FC], F32, kind="ExternalInput").ap()
    nwo_in = nc.dram_tensor("nwo", [128, FC], F32, kind="ExternalInput").ap()
    nwp_in = nc.dram_tensor("nwp", [128, FC], F32, kind="ExternalInput").ap()
    srow_in = nc.dram_tensor("srow", [1, H], F32, kind="ExternalInput").ap()
    cst_in = nc.dram_tensor("cst", [128, 512], F32, kind="ExternalInput").ap()
    sel8_in = nc.dram_tensor("sel8", [8, 1024], BF16, kind="ExternalInput").ap()
    idnb_in = nc.dram_tensor("idnb", [128, 128], BF16, kind="ExternalInput").ap()
    idnf_in = nc.dram_tensor("idnf", [128, 128], F32, kind="ExternalInput").ap()
    onesb_in = nc.dram_tensor("onesb", [128, 1], BF16, kind="ExternalInput").ap()
    onesf_in = nc.dram_tensor("onesf", [1, 128], F32, kind="ExternalInput").ap()

    yT = nc.dram_tensor("yT", [SPC, D, SEQ], F32, kind="ExternalOutput").ap()

    wqq_d = nc.dram_tensor("wqq_d", [48, FC, 128, 128], BF16).ap()
    wqp_d = nc.dram_tensor("wqp_d", [16, FC, 128, 128], BF16).ap()
    w1b_d = nc.dram_tensor("w1b_d", [NE, HC, FC, 128, 128], BF16).ap()
    w3b_d = nc.dram_tensor("w3b_d", [NE, HC, FC, 128, 128], BF16).ap()
    w2b_d = nc.dram_tensor("w2b_d", [16, NE, HC, 128, 128], BF16).ap()
    ws1b_d = nc.dram_tensor("ws1b_d", [HC, FC, 128, 128], BF16).ap()
    ws3b_d = nc.dram_tensor("ws3b_d", [HC, FC, 128, 128], BF16).ap()
    ws2b_d = nc.dram_tensor("ws2b_d", [16, HC, 128, 128], BF16).ap()

    em = Em()
    for s in ("cPE", "cDV", "cAC", "cPL"):
        em.sem(s, 1)
    for s in ("sW", "sW2", "sO", "sC", "sQ", "sQW", "sX", "sK", "sB"):
        em.sem(s, 16)

    ctx = contextlib.ExitStack()
    with ctx:
        sb = lambda n, s, d: ctx.enter_context(nc.sbuf_tensor(n, s, d))
        psum = lambda n, s, d: ctx.enter_context(nc.psum_tensor(n, s, d))

        x_sb = sb("x_sb", [128, FC * TILE], BF16)
        xn_sb = sb("xn_sb", [128, FC * TILE], BF16)
        q_sb = sb("q_sb", [128, H * TILE], BF16)
        k_sb = sb("k_sb", [128, H * TILE], BF16)
        v_sb = sb("v_sb", [128, H * TILE], BF16)
        o_sb = sb("o_sb", [128, H * TILE], BF16)
        with contextlib.ExitStack() as qctx:
            qsrc = [qctx.enter_context(nc.sbuf_tensor(f"qsrc{i}", [128, 2048], F32))
                    for i in range(2)]
            qge = qctx.enter_context(nc.sbuf_tensor("qge", [128, 2048], F32))
            cst_sb = qctx.enter_context(nc.sbuf_tensor("cst_sb", [128, 512], F32))
            tm_sb = qctx.enter_context(nc.sbuf_tensor("tm_sb", [128, 128], F32))
        rt_sb = sb("rt_sb", [128, 16 * TILE], BF16)
        A_sb = sb("A_sb", [128, 24 * TILE], BF16)
        Sb_sb = sb("Sb_sb", [128, H * 128], BF16)
        mask_sb = sb("mask_sb", [128, H * 128], BF16)
        lam_sb = sb("lam_sb", [128, H * 128], BF16)
        ring = [sb(f"ring{i}", [128, 2048], BF16) for i in range(4)]
        ktmg = sb("ktmg", [128, 1024], BF16)
        vtmg = sb("vtmg", [128, 1024], BF16)
        qlmg = sb("qlmg", [128, 1024], BF16)
        scmg = sb("scmg", [128, 1024], BF16)
        stage = [sb(f"stage{i}", [128, 512], F32) for i in range(2)]
        u_s = [sb(f"u{i}", [128, 512], BF16) for i in range(2)]
        g3_s = [sb(f"g3{i}", [128, 512], BF16) for i in range(2)]
        t5_s = [sb(f"t5{i}", [128, 512], BF16) for i in range(2)]
        sgb = [sb(f"sgb{i}", [128, 512], BF16) for i in range(2)]
        t1_sb = sb("t1_sb", [128, 512], F32)
        sq_s = [sb(f"sq{i}", [128, 512], BF16) for i in range(2)]
        bm_sb = sb("bm_sb", [128, 512], F32)
        rho_sb = sb("rho_sb", [128, 512], F32)
        rhoa_sb = sb("rhoa_sb", [128, 512], F32)
        st = sb("st", [1, 2048], F32)
        sig_fb = sb("sig_fb", [8, 512], BF16)
        sel8_sb = sb("sel8_sb", [8, 1024], BF16)
        idn_b = sb("idn_b", [128, 128], BF16)
        idn_f = sb("idn_f", [128, 128], F32)
        onesb_sb = sb("onesb_sb", [128, 1], BF16)
        onesf_sb = sb("onesf_sb", [1, 128], F32)
        gate_sb = sb("gate_sb", [128, FC * NE], BF16)
        nwq_c = sb("nwq_c", [128, FC], F32)
        nwo_c = sb("nwo_c", [128, FC], F32)
        nwp_c = sb("nwp_c", [128, FC], F32)
        nwpp_c = sb("nwpp_c", [128, FC], F32)
        s2l_c = sb("s2l_c", [128, FC], BF16)
        lamr_c = sb("lamr_c", [128, H], F32)
        lamT_c = sb("lamT_c", [128, H], F32)
        negs_c = sb("negs_c", [128, H], F32)
        la_c = sb("la_c", [128, H], F32)
        mc_c = sb("mc_c", [128, 8], F32)
        srow_sb = sb("srow_sb", [1, 16], F32)
        lg_sb = sb("lg_sb", [128, 32], F32)
        sc_sb = sb("sc_sb", [128, 32], F32)
        mx_sb = sb("mx_sb", [128, 32], F32)
        sgt_sb = sb("sgt_sb", [128, 32], F32)

        P = [psum(f"P{i}", [128, 512], F32) for i in range(7)]
        P7 = psum("P7", [128, 1024], BF16)

        sems = {}
        for name in list(em.cnt):
            sems[name] = ctx.enter_context(nc.semaphore(name))

        stv = lambda i, w=512: st[0:1, i * 512:i * 512 + w]

        def pe(fn, inc=None, waits=()):
            return em.op("PE", fn, inc, waits)

        def dv(fn, inc=None, waits=()):
            return em.op("DV", fn, inc, waits)

        def acm(fn, inc=None, waits=()):
            return em.op("AC", fn, inc, waits)

        def gp(fn, inc=None, waits=()):
            return em.op("GP", fn, inc, waits)

        def sy(fn, inc=None, waits=()):
            return em.op("SY", fn, inc, waits)

        ring_rel = [[], [], [], []]
        ring_n = [0]

        def ring_load(src_ap3, width=2048, extra=()):
            slot = ring_n[0] % 4
            ring_n[0] += 1
            dst = ring[slot][:, 0:width].rearrange("p (a b) -> p a b", b=128)
            n = sy(call("dma_start", dst, src_ap3), inc="sW",
                   waits=tuple(ring_rel[slot]) + tuple(extra))
            ring_rel[slot] = []
            return slot, ("sW", n)

        def ring_load_cast(srcf32_ap3, wb_ap3, width=2048):
            slot = ring_n[0] % 4
            ring_n[0] += 1
            dst = ring[slot][:, 0:width].rearrange("p (a b) -> p a b", b=128)
            n = gp(call("dma_start", dst, srcf32_ap3), inc="sK",
                   waits=tuple(ring_rel[slot]))
            nb = sy(call("dma_start", wb_ap3,
                         ring[slot][:, 0:width].rearrange(
                             "p (a b) -> p a b", b=128)), inc="sB",
                    waits=(("sK", n),))
            ring_rel[slot] = [("sB", nb)]
            return slot, ("sK", n)

        bank_ev = {}

        def bank_wait(i):
            return bank_ev.get(i, (None, 0))

        def ckpt(name):
            if stop_stage == name:
                raise _Stop()

        dv(call("memset", mc_c[:, :], 0.0))

        def cload(dst, src):
            return sy(call("dma_start", dst, src), inc="sC")

        cload(cst_sb[:, :], cst_in)
        cload(sel8_sb[:, :], sel8_in)
        cload(idn_b[:, :], idnb_in)
        cload(idn_f[:, :], idnf_in)
        cload(onesb_sb[:, :], onesb_in)
        cload(onesf_sb[:, :], onesf_in)
        cload(nwq_c[:, :], nwq_in)
        cload(nwo_c[:, :], nwo_in)
        cload(nwp_c[:, :], nwp_in)
        nC = cload(srow_sb[:, :], srow_in)

        def x_load(tile, wait_ev):
            seq, t0 = tile // 4, (tile % 4) * TILE
            src = xT[seq, :, t0:t0 + TILE].rearrange("(fc p) t -> p fc t", p=128)
            dst = x_sb[:, :].rearrange("p (fc t) -> p fc t", fc=FC)
            return gp(call("dma_start", dst, src), inc="sX",
                      waits=(("cDV", wait_ev),))

        x_load(0, 0)

        nPE = pe(call("matmul", P[5][:, 0:16], onesf_sb[0:1, :],
                      srow_sb[0:1, :], start=True, stop=True),
                 inc="cPE", waits=(("sC", nC),))
        nDV = dv(call("tensor_scalar", negs_c[:, :], P[5][:, 0:16], -1.0,
                      None, op0=ALU.mult), inc="cDV",
                 waits=(("cPE", nPE),))
        bank_ev[5] = ("cDV", nDV)
        dv(call("drain"))
        acm(call("activation", lamT_c[:, :], negs_c[:, :], AF.Exp,
                 scale=128.0), inc="cAC", waits=(("cDV", nDV),))
        nDV2 = dv(call("tensor_tensor", la_c[:, :], cst_sb[:, 384:400],
                       negs_c[:, :], op=ALU.mult), inc="cDV")
        dv(call("drain"))
        acm(call("activation", lamr_c[:, :], la_c[:, :], AF.Exp), inc="cAC",
            waits=(("cDV", nDV2),))

        ev_mA = 0
        ev_mB = 0
        for h in range(H):
            nA = dv(call("scalar_tensor_tensor", t1_sb[:, 0:128],
                         cst_sb[:, 0:128], negs_c[:, h:h + 1],
                         cst_sb[:, 128:256], op0=ALU.mult, op1=ALU.subtract),
                    inc="cDV", waits=(("cDV", nDV), ("cAC", ev_mA)))
            ev_mA = acm(call("activation", mask_sb[:, h * 128:(h + 1) * 128],
                             t1_sb[:, 0:128], AF.Exp), inc="cAC",
                        waits=(("cDV", nA),))
            nB = dv(call("tensor_scalar", tm_sb[:, 0:128],
                         cst_sb[:, 256:384], negs_c[:, h:h + 1], None,
                         op0=ALU.mult), inc="cDV", waits=(("cAC", ev_mB),))
            ev_mB = acm(call("activation", lam_sb[:, h * 128:(h + 1) * 128],
                             tm_sb[:, 0:128], AF.Exp), inc="cAC",
                        waits=(("cDV", nB),))
        ev_masks = max(ev_mA, ev_mB)

        dv(call("tensor_tensor", nwpp_c[:, :], nwo_c[:, :], nwp_c[:, :],
                op=ALU.mult), waits=(("sC", nC),))
        ev_s2l = dv(call("tensor_tensor", s2l_c[:, :], nwo_c[:, :],
                         nwo_c[:, :], op=ALU.mult), inc="cDV")
        ev_nwpp = ev_s2l

        qsrc_rel = [[], []]
        qge_rel = []

        def quant(n_oc, wsrc, wdst, mccol, nelem):
            last_pe = 0
            for oc in range(n_oc):
                slot = oc % 2
                src = wsrc[oc].rearrange("fc p o -> p fc o")
                qdst = qsrc[slot][:, :].rearrange("p (a b) -> p a b", b=128)
                nL = sy(call("dma_start", qdst, src), inc="sQ",
                        waits=tuple(qsrc_rel[slot]))
                qsrc_rel[slot] = []
                nA = acm(call("activation", ring[slot][:, :],
                              qsrc[slot][:, :], AF.Abs), inc="cAC",
                         waits=(("sQ", nL),) + tuple(ring_rel[slot]))
                ring_rel[slot] = []
                qsrc_rel[slot].append(("cAC", nA))
                for b in range(4):
                    last_pe = pe(call("matmul", P[6][0:1, 0:512],
                                      onesb_sb[:, :],
                                      ring[slot][:, b * 512:(b + 1) * 512],
                                      start=(oc == 0 and b == 0),
                                      stop=(oc == n_oc - 1 and b == 3),
                                      skip_group_check=True), inc="cPE",
                                 waits=(("cAC", nA), bank_wait(6)))
                ring_rel[slot] = [("cPE", last_pe)]
            n1 = dv(call("tensor_copy", stv(0, 512), P[6][0:1, 0:512]),
                    inc="cDV", waits=(("cPE", last_pe),))
            bank_ev[6] = ("cDV", n1)
            dv(call("tensor_reduce", st[0:1, 1024:1025], stv(0, 512),
                    axis=AX.X, op=ALU.add))
            dv(call("drain"))
            dv(call("tensor_scalar", st[0:1, 1026:1027], st[0:1, 1024:1025],
                    1.0 / nelem, None, op0=ALU.mult))
            dv(call("drain"))
            dv(call("tensor_scalar", st[0:1, 1026:1027], st[0:1, 1026:1027],
                    1e-5, None, op0=ALU.max))
            dv(call("drain"))
            dv(call("tensor_scalar", st[0:1, 1028:1029], st[0:1, 1026:1027],
                    0.5, None, op0=ALU.mult))
            dv(call("drain"))
            dv(call("tensor_scalar", st[0:1, 1029:1030], st[0:1, 1028:1029],
                    -1.0, None, op0=ALU.mult))
            dv(call("drain"))
            n3 = dv(call("tensor_copy", st[0:1, 1030:1031],
                         st[0:1, 1026:1027]), inc="cDV")
            nP5 = pe(call("matmul", P[5][:, 0:3], onesf_sb[0:1, :],
                          st[0:1, 1028:1031], start=True, stop=True),
                     inc="cPE", waits=(("cDV", n3), bank_wait(5)))
            nmc = dv(call("tensor_copy", mc_c[:, mccol:mccol + 3],
                          P[5][:, 0:3]), inc="cDV", waits=(("cPE", nP5),))
            bank_ev[5] = ("cDV", nmc)
            dv(call("drain"))
            last_w = 0
            for oc in range(n_oc):
                slot = oc % 2
                src = wsrc[oc].rearrange("fc p o -> p fc o")
                qdst = qsrc[slot][:, :].rearrange("p (a b) -> p a b", b=128)
                nL = sy(call("dma_start", qdst, src), inc="sQ",
                        waits=tuple(qsrc_rel[slot]))
                qsrc_rel[slot] = []
                nG = dv(call("tensor_scalar", qge[:, :], qsrc[slot][:, :],
                             mc_c[:, mccol + 1:mccol + 2], None,
                             op0=ALU.is_le), inc="cDV",
                        waits=(("sQ", nL), ("cDV", nmc)) + tuple(qge_rel))
                qge_rel.clear()
                rslot = 2 + (oc % 2)
                nT = dv(call("scalar_tensor_tensor", ring[rslot][:, :],
                             qsrc[slot][:, :], mc_c[:, mccol:mccol + 1],
                             qge[:, :], op0=ALU.is_ge, op1=ALU.subtract),
                        inc="cDV", waits=tuple(ring_rel[rslot]))
                ring_rel[rslot] = []
                qsrc_rel[slot].append(("cDV", nT))
                qge_rel.append(("cDV", nT))
                dst = wdst[oc].rearrange("fc p o -> p fc o")
                last_w = sy(call(
                    "dma_start", dst,
                    ring[rslot][:, :].rearrange("p (a b) -> p a b", b=128)),
                    inc="sQW", waits=(("cDV", nT),))
                ring_rel[rslot] = [("sQW", last_w)]
            return last_w, nmc

        nQW_qkv, nmc_q = quant(48, wqkvT, wqq_d, 0, float(3 * D * D))
        nQW_proj, nmc_p = quant(16, wprojT, wqp_d, 4, float(D * D))
        nmc_done = max(nmc_q, nmc_p)

        kev_gate = gp(call("dma_start",
                           gate_sb[:, :].rearrange("p (a b) -> p a b", b=NE),
                           gT.rearrange("fc p e -> p fc e")), inc="sK")

        ev_x_free = 0
        ev_xn_prev_rd = 0
        ev_rt_read = 0
        ev_A_free = 0
        ev_stage = [0, 0]
        ev_u = [0, 0]
        ev_g3 = [0, 0]
        ev_t5 = [0, 0]
        ev_sq = [0, 0]
        ev_sig_rd = 0
        ev_sig = 0
        ev_sig_consumed = 0
        ev_P7 = 0
        ev_scm = [0, 0]
        ev_ktv = [0, 0]
        ev_qlm = [0, 0]
        ev_S = {}
        ev_o_evac = 0
        ev_z_prev = 0
        ev_qkv_war_PE = 0
        ev_qkv_war_PL = 0
        ev_qkv_evac = {}

        def ln_stats(gate_waits):
            last_st = 0
            for fc in range(FC):
                xsl = x_sb[:, fc * TILE:(fc + 1) * TILE]
                sqs = sq_s[fc % 2]
                if fc % 2 == 0:
                    nq = acm(call("activation", sqs[:, :], xsl, AF.Square),
                             inc="cAC",
                             waits=tuple(gate_waits) +
                             (("cPE", ev_sq[fc % 2]),))
                    sqw = ("cAC", nq)
                else:
                    nq = dv(call("tensor_tensor", sqs[:, :], xsl, xsl,
                                 op=ALU.mult), inc="cDV",
                            waits=tuple(gate_waits) +
                            (("cPE", ev_sq[fc % 2]),))
                    sqw = ("cDV", nq)
                pe(call("matmul", P[6][0:1, 0:512], onesb_sb[:, :], xsl,
                        start=(fc == 0), stop=(fc == FC - 1),
                        skip_group_check=True),
                   waits=tuple(gate_waits) + (bank_wait(6),))
                last_st = pe(call("matmul", P[5][0:1, 0:512], onesb_sb[:, :],
                                  sqs[:, :], start=(fc == 0),
                                  stop=(fc == FC - 1),
                                  skip_group_check=True), inc="cPE",
                             waits=(sqw, bank_wait(5)))
                ev_sq[fc % 2] = last_st
            return last_st

        def ln_rows(last_st, eps, fold_eps):
            n0 = dv(call("tensor_scalar", stv(0), P[6][0:1, 0:512], 1.0 / D,
                         None, op0=ALU.mult), inc="cDV",
                    waits=(("cPE", last_st),))
            bank_ev[6] = ("cDV", n0)
            n5 = dv(call("tensor_scalar", stv(1), P[5][0:1, 0:512], 1.0 / D,
                         None, op0=ALU.mult), inc="cDV")
            bank_ev[5] = ("cDV", n5)
            dv(call("tensor_tensor", stv(3), stv(0), stv(0), op=ALU.mult))
            dv(call("tensor_tensor", stv(1), stv(1), stv(3), op=ALU.subtract))
            nv = dv(call("tensor_scalar", stv(2), stv(1), eps, None,
                         op0=ALU.add), inc="cDV")
            acm(call("activation", stv(2), stv(2), AF.Ln), inc="cAC",
                waits=(("cDV", nv),))
            nr = acm(call("activation", stv(2), stv(2), AF.Exp, scale=-0.5),
                     inc="cAC")
            if fold_eps is None:
                return nr
            nf = dv(call("tensor_tensor", stv(3), stv(2), stv(2),
                         op=ALU.mult), inc="cDV", waits=(("cAC", nr),))
            dv(call("tensor_tensor", stv(3), stv(1), stv(3), op=ALU.mult))
            nf = dv(call("tensor_scalar", stv(3), stv(3), fold_eps, None,
                         op0=ALU.add), inc="cDV")
            acm(call("activation", stv(3), stv(3), AF.Ln), inc="cAC",
                waits=(("cDV", nf),))
            na = acm(call("activation", stv(3), stv(3), AF.Exp, scale=-0.5),
                     inc="cAC")
            nrho = dv(call("tensor_tensor", stv(2), stv(2), stv(3),
                           op=ALU.mult), inc="cDV", waits=(("cAC", na),))
            return nrho

        def bcast2(row_m, row_r, rows_done):
            nbm = pe(call("matmul", P[5][:, 0:512], onesf_sb[0:1, :], row_m,
                          start=True, stop=True), inc="cPE",
                     waits=(rows_done, bank_wait(5)))
            nbe = dv(call("tensor_copy", bm_sb[:, :], P[5][:, 0:512]),
                     inc="cDV", waits=(("cPE", nbm),))
            bank_ev[5] = ("cDV", nbe)
            nrh = pe(call("matmul", P[6][:, 0:512], onesf_sb[0:1, :], row_r,
                          start=True, stop=True), inc="cPE",
                     waits=(bank_wait(6),))
            nre = dv(call("tensor_copy", rho_sb[:, :], P[6][:, 0:512]),
                     inc="cDV", waits=(("cPE", nrh),))
            bank_ev[6] = ("cDV", nre)
            return nbe, nre

        try:
          for tile in range(n_tiles):
            seq, t0 = tile // 4, (tile % 4) * TILE
            if tile > 0:
                x_load(tile, ev_x_free)
            nXl = em.cnt["sX"]
            ckpt("xload")

            last_st = ln_stats([("sX", nXl)])
            nrho = ln_rows(last_st, EPS_LN, EPSF)
            nbe, nre = bcast2(stv(0), stv(2), ("cDV", nrho))
            ev_xn_fc = []
            for fc in range(FC):
                xsl = x_sb[:, fc * TILE:(fc + 1) * TILE]
                dv(call("tensor_tensor", t1_sb[:, :], xsl, bm_sb[:, :],
                        op=ALU.subtract),
                   waits=(("cDV", nbe), ("cPE", ev_xn_prev_rd),
                          ("cAC", ev_masks)))
                ev_xn_fc.append(dv(call("scalar_tensor_tensor",
                                        xn_sb[:, fc * TILE:(fc + 1) * TILE],
                                        t1_sb[:, :], nwq_c[:, fc:fc + 1],
                                        rho_sb[:, :], op0=ALU.mult,
                                        op1=ALU.mult),
                                   inc="cDV", waits=(("cDV", nre),)))
            ev_xn = ev_xn_fc[-1]
            ckpt("ln1")

            for oc in range(48):
                src = wqq_d[oc].rearrange("fc p o -> p fc o")
                extra = (("sQW", 16 * (oc + 1)),) if tile == 0 else ()
                slot, nW = ring_load(src, extra=extra)
                pacc = P[oc % 2]
                last = 0
                for fcK in range(FC):
                    last = pe(call("matmul", pacc[:, 0:512],
                                   ring[slot][:, fcK * 128:(fcK + 1) * 128],
                                   xn_sb[:, fcK * TILE:(fcK + 1) * TILE],
                                   start=(fcK == 0), stop=(fcK == FC - 1)),
                              inc=("cPE" if fcK == FC - 1 else None),
                              waits=(nW, ("cDV", ev_xn_fc[fcK]),
                                     bank_wait(oc % 2)))
                ring_rel[slot] = [("cPE", last)]
                if oc < 16:
                    dstb, o2 = q_sb, oc
                elif oc < 32:
                    dstb, o2 = k_sb, oc - 16
                else:
                    dstb, o2 = v_sb, oc - 32
                nE = acm(call("activation",
                              dstb[:, o2 * TILE:(o2 + 1) * TILE],
                              pacc[:, 0:512], AF.Copy, scale=mc_c[:, 2:3]),
                         inc="cAC",
                         waits=(("cPE", last), ("cDV", nmc_done),
                                ("cPE", ev_qkv_war_PE),
                                ("cPL", ev_qkv_war_PL)))
                bank_ev[oc % 2] = ("cAC", nE)
                ev_qkv_evac[oc] = nE
            ev_qkv_pe = em.cnt["cPE"]
            ckpt("qkv")

            for cc in range(4):
                gch = (tile % 4) * 4 + cc
                first = (gch == 0)
                tc = cc * 128
                for hg in range(4):
                    h0 = hg * 4
                    gslot = hg % 2
                    ntr = 0
                    for j in range(4):
                        h = h0 + j
                        pe(call("transpose", P7[:, j * 128:(j + 1) * 128],
                                k_sb[:, h * TILE + tc:h * TILE + tc + 128],
                                idn_b[:, :]),
                           waits=(("cAC", ev_qkv_evac[16 + h]),
                                  ("cAC", ev_P7)))
                    for j in range(4):
                        h = h0 + j
                        ntr = pe(call("transpose",
                                      P7[:, 512 + j * 128:512 + (j + 1) * 128],
                                      v_sb[:, h * TILE + tc:h * TILE + tc + 128],
                                      idn_b[:, :]),
                                 inc=("cPE" if j == 3 else None),
                                 waits=(("cAC", ev_qkv_evac[32 + h]),))
                    for j in range(4):
                        h = h0 + j
                        acm(call("activation",
                                 ktmg[:, gslot * 512 + j * 128:
                                      gslot * 512 + (j + 1) * 128],
                                 P7[:, j * 128:(j + 1) * 128], AF.Copy,
                                 scale=lamr_c[:, h:h + 1]),
                            waits=(("cPE", ntr), ("cPE", ev_ktv[gslot])))
                    nkv = 0
                    for j in range(4):
                        nkv = acm(call("activation",
                                       vtmg[:, gslot * 512 + j * 128:
                                            gslot * 512 + (j + 1) * 128],
                                       P7[:, 512 + j * 128:512 + (j + 1) * 128],
                                       AF.Copy),
                                  inc=("cAC" if j == 3 else None))
                    ev_P7 = nkv
                    nql = 0
                    for j in range(4):
                        h = h0 + j
                        nql = gp(call("tensor_tensor",
                                      qlmg[:, gslot * 512 + j * 128:
                                           gslot * 512 + (j + 1) * 128],
                                      q_sb[:, h * TILE + tc:h * TILE + tc + 128],
                                      lam_sb[:, h * 128:(h + 1) * 128],
                                      op=ALU.mult),
                                 inc=("cPL" if j == 3 else None),
                                 waits=(("cAC", ev_qkv_evac[h]),
                                        ("cAC", ev_masks),
                                        ("cPE", ev_qlm[gslot])))
                    nsc = 0
                    for j in range(4):
                        h = h0 + j
                        nsc = pe(call("matmul",
                                      P[2][:, j * 128:(j + 1) * 128],
                                      k_sb[:, h * TILE + tc:h * TILE + tc + 128],
                                      q_sb[:, h * TILE + tc:h * TILE + tc + 128],
                                      start=(j == 0), stop=(j == 3),
                                      skip_group_check=True),
                                 inc=("cPE" if j == 3 else None),
                                 waits=(("cAC", ev_qkv_evac[16 + h]),
                                        ("cAC", ev_qkv_evac[h]),
                                        bank_wait(2)))
                    nscm = 0
                    for j in range(4):
                        h = h0 + j
                        nscm = dv(call("tensor_tensor",
                                       scmg[:, gslot * 512 + j * 128:
                                            gslot * 512 + (j + 1) * 128],
                                       P[2][:, j * 128:(j + 1) * 128],
                                       mask_sb[:, h * 128:(h + 1) * 128],
                                       op=ALU.mult),
                                  inc=("cDV" if j == 3 else None),
                                  waits=(("cPE", nsc), ("cAC", ev_masks),
                                         ("cPE", ev_scm[gslot])))
                    bank_ev[2] = ("cDV", nscm)
                    no = 0
                    for j in range(4):
                        no = pe(call("matmul", P[3][:, j * 128:(j + 1) * 128],
                                     vtmg[:, gslot * 512 + j * 128:
                                          gslot * 512 + (j + 1) * 128],
                                     scmg[:, gslot * 512 + j * 128:
                                          gslot * 512 + (j + 1) * 128],
                                     start=(j == 0),
                                     stop=(first and j == 3),
                                     skip_group_check=True),
                                inc=("cPE" if (first and j == 3) else None),
                                waits=(("cAC", nkv), ("cDV", nscm),
                                       ("cAC", ev_o_evac)))
                    if not first:
                        for j in range(4):
                            h = h0 + j
                            no = pe(call("matmul",
                                         P[3][:, j * 128:(j + 1) * 128],
                                         Sb_sb[:, h * 128:(h + 1) * 128],
                                         qlmg[:, gslot * 512 + j * 128:
                                              gslot * 512 + (j + 1) * 128],
                                         start=False, stop=(j == 3),
                                         skip_group_check=True),
                                    inc=("cPE" if j == 3 else None),
                                    waits=(("cPL", nql),
                                           ("cDV", ev_S.get((gch - 1, hg),
                                                            0))))
                    ev_scm[gslot] = no
                    ev_qlm[gslot] = no
                    ov = o_sb[:, :].rearrange("p (h t) -> p h t", h=H)
                    ev_o_evac = acm(call("activation",
                                         ov[:, h0:h0 + 4, tc:tc + 128],
                                         P[3][:, 0:512].rearrange(
                                             "p (a b) -> p a b", a=4),
                                         AF.Copy, scale=SCALE_ATT),
                                    inc="cAC",
                                    waits=(("cPE", no), ("cDV", ev_z_prev)))
                    bank_ev[3] = ("cAC", ev_o_evac)
                    nkvm = 0
                    for j in range(4):
                        nkvm = pe(call("matmul",
                                       P[4][:, j * 128:(j + 1) * 128],
                                       ktmg[:, gslot * 512 + j * 128:
                                            gslot * 512 + (j + 1) * 128],
                                       vtmg[:, gslot * 512 + j * 128:
                                            gslot * 512 + (j + 1) * 128],
                                       start=(j == 0), stop=(j == 3),
                                       skip_group_check=True),
                                  inc=("cPE" if j == 3 else None),
                                  waits=(("cAC", nkv), bank_wait(4)))
                    ev_ktv[gslot] = nkvm
                    nS = 0
                    for j in range(4):
                        h = h0 + j
                        if first:
                            nS = dv(call("tensor_copy",
                                         Sb_sb[:, h * 128:(h + 1) * 128],
                                         P[4][:, j * 128:(j + 1) * 128]),
                                    inc=("cDV" if j == 3 else None),
                                    waits=(("cPE", nkvm),))
                        else:
                            nS = dv(call("scalar_tensor_tensor",
                                         Sb_sb[:, h * 128:(h + 1) * 128],
                                         Sb_sb[:, h * 128:(h + 1) * 128],
                                         lamT_c[:, h:h + 1],
                                         P[4][:, j * 128:(j + 1) * 128],
                                         op0=ALU.mult, op1=ALU.add),
                                    inc=("cDV" if j == 3 else None),
                                    waits=(("cPE", nkvm),))
                    bank_ev[4] = ("cDV", nS)
                    ev_S[(gch, hg)] = nS
            ev_qkv_war_PE = em.cnt["cPE"]
            ev_qkv_war_PL = em.cnt["cPL"]
            ckpt("attn")

            last_st = 0
            for fc in range(FC):
                osl = o_sb[:, fc * TILE:(fc + 1) * TILE]
                sqs = sq_s[fc % 2]
                if fc % 2 == 0:
                    nq = acm(call("activation", sqs[:, :], osl, AF.Square),
                             inc="cAC",
                             waits=(("cAC", ev_o_evac),
                                    ("cPE", ev_sq[fc % 2])))
                    sqw = ("cAC", nq)
                else:
                    nq = dv(call("tensor_tensor", sqs[:, :], osl, osl,
                                 op=ALU.mult), inc="cDV",
                            waits=(("cAC", ev_o_evac),
                                   ("cPE", ev_sq[fc % 2])))
                    sqw = ("cDV", nq)
                pe(call("matmul", P[6][0:1, 0:512], onesb_sb[:, :],
                        sqs[:, :], start=(fc == 0), stop=(fc == FC - 1),
                        skip_group_check=True),
                   waits=(sqw, bank_wait(6)))
                last_st = pe(call("matmul", P[5][0:1, 0:512],
                                  s2l_c[:, fc:fc + 1], sqs[:, :],
                                  start=(fc == 0), stop=(fc == FC - 1),
                                  skip_group_check=True), inc="cPE",
                             waits=(("cDV", ev_s2l), bank_wait(5)))
                ev_sq[fc % 2] = last_st
            n0 = dv(call("tensor_scalar", stv(0), P[6][0:1, 0:512], 1.0 / D,
                         None, op0=ALU.mult), inc="cDV",
                    waits=(("cPE", last_st),))
            bank_ev[6] = ("cDV", n0)
            n5 = dv(call("tensor_scalar", stv(1), P[5][0:1, 0:512], 1.0 / D,
                         None, op0=ALU.mult), inc="cDV")
            bank_ev[5] = ("cDV", n5)
            nv = dv(call("tensor_scalar", stv(2), stv(0), 1e-6, None,
                         op0=ALU.add), inc="cDV")
            acm(call("activation", stv(2), stv(2), AF.Ln), inc="cAC",
                waits=(("cDV", nv),))
            nr = acm(call("activation", stv(2), stv(2), AF.Exp, scale=-0.5),
                     inc="cAC")
            nf = dv(call("tensor_tensor", stv(3), stv(2), stv(2),
                         op=ALU.mult), inc="cDV", waits=(("cAC", nr),))
            dv(call("tensor_tensor", stv(3), stv(1), stv(3), op=ALU.mult))
            nf = dv(call("tensor_scalar", stv(3), stv(3), EPSF, None,
                         op0=ALU.add), inc="cDV")
            acm(call("activation", stv(3), stv(3), AF.Ln), inc="cAC",
                waits=(("cDV", nf),))
            na = acm(call("activation", stv(3), stv(3), AF.Exp, scale=-0.5),
                     inc="cAC")
            nf = dv(call("tensor_tensor", stv(2), stv(2), stv(3),
                         op=ALU.mult), inc="cDV", waits=(("cAC", na),))
            nrh = pe(call("matmul", P[5][:, 0:512], onesf_sb[0:1, :], stv(2),
                          start=True, stop=True), inc="cPE",
                     waits=(("cDV", nf), bank_wait(5)))
            nra = dv(call("tensor_copy", rhoa_sb[:, :], P[5][:, 0:512]),
                     inc="cDV", waits=(("cPE", nrh),))
            bank_ev[5] = ("cDV", nra)

            ev_z_fc = []
            for fc in range(FC):
                ev_z_fc.append(dv(call("scalar_tensor_tensor",
                                       xn_sb[:, fc * TILE:(fc + 1) * TILE],
                                       o_sb[:, fc * TILE:(fc + 1) * TILE],
                                       nwpp_c[:, fc:fc + 1], rhoa_sb[:, :],
                                       op0=ALU.mult, op1=ALU.mult),
                                  inc="cDV",
                                  waits=(("cDV", nra), ("cPE", ev_qkv_pe),
                                         ("cDV", ev_nwpp))))
            ev_z = ev_z_fc[-1]
            ev_z_prev = ev_z

            for oc in range(16):
                src = wqp_d[oc].rearrange("fc p o -> p fc o")
                extra = ((("sQW", nQW_qkv + 16 * (oc + 1)),)
                         if tile == 0 else ())
                slot, nW = ring_load(src, extra=extra)
                pacc = P[oc % 2]
                last = 0
                for fcK in range(FC):
                    last = pe(call("matmul", pacc[:, 0:512],
                                   ring[slot][:, fcK * 128:(fcK + 1) * 128],
                                   xn_sb[:, fcK * TILE:(fcK + 1) * TILE],
                                   start=(fcK == 0), stop=(fcK == FC - 1)),
                              inc=("cPE" if fcK == FC - 1 else None),
                              waits=(nW, ("cDV", ev_z_fc[fcK]),
                                     bank_wait(oc % 2)))
                ring_rel[slot] = [("cPE", last)]
                nE = dv(call("scalar_tensor_tensor",
                             x_sb[:, oc * TILE:(oc + 1) * TILE],
                             pacc[:, 0:512], mc_c[:, 6:7],
                             x_sb[:, oc * TILE:(oc + 1) * TILE],
                             op0=ALU.mult, op1=ALU.add), inc="cDV",
                        waits=(("cPE", last), ("cDV", nmc_done)))
                bank_ev[oc % 2] = ("cDV", nE)
            ev_proj_pe = em.cnt["cPE"]
            ev_x2 = em.cnt["cDV"]
            ckpt("proj")

            last_st = ln_stats([("cDV", ev_x2)])
            nr2 = ln_rows(last_st, EPS_LN, None)
            nbe, nre = bcast2(stv(0), stv(2), ("cAC", nr2))
            ev_xn2_fc = []
            for fc in range(FC):
                xsl = x_sb[:, fc * TILE:(fc + 1) * TILE]
                dv(call("tensor_tensor", t1_sb[:, :], xsl, bm_sb[:, :],
                        op=ALU.subtract), waits=(("cDV", nbe),))
                ev_xn2_fc.append(dv(call("tensor_tensor",
                                         xn_sb[:, fc * TILE:(fc + 1) * TILE],
                                         t1_sb[:, :], rho_sb[:, :],
                                         op=ALU.mult), inc="cDV",
                                    waits=(("cDV", nre),
                                           ("cPE", ev_proj_pe))))
            ev_xn_full = ev_xn2_fc[-1]
            ckpt("ln2")

            ngl = 0
            for tt in range(4):
                for fcK in range(FC):
                    ngl = pe(call("matmul", P[6][:, tt * 8:(tt + 1) * 8],
                                  xn_sb[:, fcK * TILE + tt * 128:
                                        fcK * TILE + tt * 128 + 128],
                                  gate_sb[:, fcK * 8:(fcK + 1) * 8],
                                  start=(tt == 0 and fcK == 0),
                                  stop=(fcK == FC - 1),
                                  skip_group_check=True),
                             inc=("cPE" if (fcK == FC - 1 and tt == 3)
                                  else None),
                             waits=(("cDV", ev_xn2_fc[fcK]),
                                    ("sK", kev_gate), bank_wait(6)))
            nlg = dv(call("tensor_copy", lg_sb[:, :], P[6][:, 0:32]),
                     inc="cDV", waits=(("cPE", ngl),))
            bank_ev[6] = ("cDV", nlg)
            nsg = acm(call("activation", sc_sb[:, :], lg_sb[:, :],
                           AF.Sigmoid), inc="cAC", waits=(("cDV", nlg),))
            for tt in range(4):
                dv(call("max", out=mx_sb[:, tt * 8:(tt + 1) * 8],
                        in_=sc_sb[:, tt * 8:(tt + 1) * 8]),
                   waits=(("cAC", nsg),))
            dv(call("drain"))
            nsig = 0
            for tt in range(4):
                nsig = dv(call("scalar_tensor_tensor",
                               sgt_sb[:, tt * 8:(tt + 1) * 8],
                               sc_sb[:, tt * 8:(tt + 1) * 8],
                               mx_sb[:, tt * 8 + 1:tt * 8 + 2],
                               sc_sb[:, tt * 8:(tt + 1) * 8],
                               op0=ALU.is_ge, op1=ALU.mult),
                          inc=("cDV" if tt == 3 else None),
                          waits=(("cPE", ev_sig_rd),))
            dv(call("drain"))
            ntr = 0
            for tt in range(4):
                ntr = pe(call("transpose", P[5][0:8, tt * 128:(tt + 1) * 128],
                              sgt_sb[:, tt * 8:(tt + 1) * 8], idn_f[:, :]),
                         inc=("cPE" if tt == 3 else None),
                         waits=(("cDV", nsig), bank_wait(5)))
            ev_sig_rd = ntr
            ev_sig = dv(call("tensor_copy", sig_fb[:, :], P[5][0:8, 0:512]),
                        inc="cDV", waits=(("cPE", ntr),
                                          ("cPE", ev_sig_consumed)))
            bank_ev[5] = ("cDV", ev_sig)
            ckpt("gate")

            for gi, exps in enumerate(EGROUPS):
                nsb = 0
                for e_ in exps:
                    shared = (e_ == 8)
                    sslot = e_ % 2
                    if not shared:
                        nbc = pe(call("matmul", P[5][:, 0:512],
                                      sel8_sb[0:8, e_ * 128:(e_ + 1) * 128],
                                      sig_fb[0:8, :], start=True, stop=True),
                                 inc="cPE",
                                 waits=(("cDV", ev_sig), bank_wait(5)))
                        nsb = dv(call("tensor_copy", sgb[sslot][:, :],
                                      P[5][:, 0:512]), inc="cDV",
                                 waits=(("cPE", nbc),))
                        bank_ev[5] = ("cDV", nsb)
                    for hc in range(HC):
                        aslot = (e_ % 2) * 8 + hc if e_ < 8 else 16 + hc
                        uslot = hc % 2
                        bk1, bk3 = (2 * hc) % 2, (2 * hc + 1) % 2
                        if tile == 0:
                            srcf = (w1T[e_, hc] if not shared else ws1T[hc]
                                    ).rearrange("fc p h -> p fc h")
                            wb = (w1b_d[e_, hc] if not shared else ws1b_d[hc]
                                  ).rearrange("fc p h -> p fc h")
                            slot, nW = ring_load_cast(srcf, wb)
                        else:
                            src = (w1b_d[e_, hc] if not shared else
                                   ws1b_d[hc]).rearrange("fc p h -> p fc h")
                            slot, nW = ring_load(
                                src, extra=(("sB", nB_all),))
                        pacc = P[bk1]
                        last = 0
                        for fcK in range(FC):
                            last = pe(call(
                                "matmul", pacc[:, 0:512],
                                ring[slot][:, fcK * 128:(fcK + 1) * 128],
                                xn_sb[:, fcK * TILE:(fcK + 1) * TILE],
                                start=(fcK == 0), stop=(fcK == FC - 1)),
                                inc=("cPE" if fcK == FC - 1 else None),
                                waits=(nW, ("cDV", ev_xn2_fc[fcK]),
                                       bank_wait(bk1)))
                        ring_rel[slot] = ring_rel[slot] + [("cPE", last)]
                        if shared:
                            nu = acm(call("activation", u_s[uslot][:, :],
                                          pacc[:, 0:512], AF.Silu),
                                     inc="cAC",
                                     waits=(("cPE", last),
                                            ("cPL", ev_u[uslot])))
                            bank_ev[bk1] = ("cAC", nu)
                        else:
                            nu0 = dv(call("tensor_tensor", u_s[uslot][:, :],
                                          pacc[:, 0:512], sgb[sslot][:, :],
                                          op=ALU.mult), inc="cDV",
                                     waits=(("cPE", last), ("cDV", nsb),
                                            ("cPL", ev_u[uslot])))
                            nu = acm(call("activation", u_s[uslot][:, :],
                                          u_s[uslot][:, :], AF.Silu),
                                     inc="cAC", waits=(("cDV", nu0),))
                            bank_ev[bk1] = ("cDV", nu0)
                        if tile == 0:
                            srcf = (w3T[e_, hc] if not shared else ws3T[hc]
                                    ).rearrange("fc p h -> p fc h")
                            wb = (w3b_d[e_, hc] if not shared else ws3b_d[hc]
                                  ).rearrange("fc p h -> p fc h")
                            slot, nW = ring_load_cast(srcf, wb)
                        else:
                            src = (w3b_d[e_, hc] if not shared else
                                   ws3b_d[hc]).rearrange("fc p h -> p fc h")
                            slot, nW = ring_load(src)
                        pacc = P[bk3]
                        last = 0
                        for fcK in range(FC):
                            last = pe(call(
                                "matmul", pacc[:, 0:512],
                                ring[slot][:, fcK * 128:(fcK + 1) * 128],
                                xn_sb[:, fcK * TILE:(fcK + 1) * TILE],
                                start=(fcK == 0), stop=(fcK == FC - 1)),
                                inc=("cPE" if fcK == FC - 1 else None),
                                waits=(nW, bank_wait(bk3)))
                        ring_rel[slot] = ring_rel[slot] + [("cPE", last)]
                        ng3 = acm(call("activation", g3_s[uslot][:, :],
                                       pacc[:, 0:512], AF.Copy), inc="cAC",
                                  waits=(("cPE", last),
                                         ("cPL", ev_g3[uslot])))
                        bank_ev[bk3] = ("cAC", ng3)
                        if shared:
                            nA = gp(call("tensor_tensor",
                                         A_sb[:, aslot * TILE:
                                              (aslot + 1) * TILE],
                                         u_s[uslot][:, :], g3_s[uslot][:, :],
                                         op=ALU.mult), inc="cPL",
                                    waits=(("cAC", nu), ("cAC", ng3),
                                           ("cPE", ev_A_free)))
                            ev_u[uslot] = nA
                            ev_g3[uslot] = nA
                        else:
                            nt5 = gp(call("tensor_tensor", t5_s[uslot][:, :],
                                          u_s[uslot][:, :], g3_s[uslot][:, :],
                                          op=ALU.mult), inc="cPL",
                                     waits=(("cAC", nu), ("cAC", ng3),
                                            ("cDV", ev_t5[uslot])))
                            ev_u[uslot] = nt5
                            ev_g3[uslot] = nt5
                            nA = dv(call("tensor_tensor",
                                         A_sb[:, aslot * TILE:
                                              (aslot + 1) * TILE],
                                         t5_s[uslot][:, :], sgb[sslot][:, :],
                                         op=ALU.mult), inc="cDV",
                                    waits=(("cPL", nt5), ("cPE", ev_A_free)))
                            ev_t5[uslot] = nA
                ev_G_done_DV = em.cnt["cDV"]
                ev_G_done_PL = em.cnt["cPL"]
                ev_sig_consumed = em.cnt["cPE"]
                ckpt("moeg")
                for oc in range(16):
                    pacc = P[oc % 2]
                    nmm = 0
                    for ei, e_ in enumerate(exps):
                        shared = (e_ == 8)
                        if tile == 0:
                            srcf = (w2T[oc, e_] if not shared else ws2T[oc]
                                    ).rearrange("hc p o -> p hc o")
                            wb = (w2b_d[oc, e_] if not shared else ws2b_d[oc]
                                  ).rearrange("hc p o -> p hc o")
                            slot, nW = ring_load_cast(srcf, wb, width=1024)
                        else:
                            src = (w2b_d[oc, e_] if not shared else
                                   ws2b_d[oc]).rearrange("hc p o -> p hc o")
                            slot, nW = ring_load(src, width=1024)
                        for hcK in range(HC):
                            aslot = (e_ % 2) * 8 + hcK if e_ < 8 else 16 + hcK
                            nmm = pe(call(
                                "matmul", pacc[:, 0:512],
                                ring[slot][:, hcK * 128:(hcK + 1) * 128],
                                A_sb[:, aslot * TILE:(aslot + 1) * TILE],
                                start=(ei == 0 and hcK == 0),
                                stop=(ei == len(exps) - 1 and hcK == HC - 1)),
                                inc=("cPE" if hcK == HC - 1 else None),
                                waits=(nW, ("cDV", ev_G_done_DV),
                                       ("cPL", ev_G_done_PL),
                                       bank_wait(oc % 2)))
                        ring_rel[slot] = ring_rel[slot] + [("cPE", nmm)]
                    if gi == 0:
                        nE = acm(call("activation",
                                      rt_sb[:, oc * TILE:(oc + 1) * TILE],
                                      pacc[:, 0:512], AF.Copy), inc="cAC",
                                 waits=(("cPE", nmm), ("cDV", ev_rt_read)))
                        bank_ev[oc % 2] = ("cAC", nE)
                    elif gi < 3:
                        nE = dv(call("tensor_tensor",
                                     rt_sb[:, oc * TILE:(oc + 1) * TILE],
                                     rt_sb[:, oc * TILE:(oc + 1) * TILE],
                                     pacc[:, 0:512], op=ALU.add), inc="cDV",
                                waits=(("cPE", nmm),))
                        bank_ev[oc % 2] = ("cDV", nE)
                    else:
                        sslot = oc % 2
                        n1 = dv(call("scalar_tensor_tensor",
                                     stage[sslot][:, :],
                                     rt_sb[:, oc * TILE:(oc + 1) * TILE],
                                     1.0, pacc[:, 0:512], op0=ALU.mult,
                                     op1=ALU.add), inc="cDV",
                                waits=(("cPE", nmm),
                                       ("sO", ev_stage[sslot])))
                        n2 = dv(call("tensor_tensor", stage[sslot][:, :],
                                     stage[sslot][:, :],
                                     x_sb[:, oc * TILE:(oc + 1) * TILE],
                                     op=ALU.add), inc="cDV")
                        bank_ev[oc % 2] = ("cDV", n1)
                        nO = sy(call("dma_start",
                                     yT[seq, oc * 128:(oc + 1) * 128,
                                        t0:t0 + TILE], stage[sslot][:, :]),
                                inc="sO", waits=(("cDV", n2),))
                        ev_stage[sslot] = nO
                ev_A_free = em.cnt["cPE"]
                if gi == 3:
                    ev_rt_read = em.cnt["cDV"]
            ev_x_free = em.cnt["cDV"]
            ev_xn_prev_rd = em.cnt["cPE"]
            if tile == 0:
                nB_all = em.cnt["sB"]
        except _Stop:
            pass

        if dump:
            all_bufs = dict(x_sb=x_sb, xn_sb=xn_sb, q_sb=q_sb, k_sb=k_sb,
                            v_sb=v_sb, o_sb=o_sb, A_sb=A_sb, rt_sb=rt_sb,
                            S_sb=Sb_sb, mask_sb=mask_sb, lam_sb=lam_sb,
                            sig_fb=sig_fb, mc_c=mc_c, sgt_sb=sgt_sb,
                            sc_sb=sc_sb, st=st, bm_sb=bm_sb, rho_sb=rho_sb,
                            rhoa_sb=rhoa_sb, lamr_c=lamr_c, lamT_c=lamT_c,
                            stage0=stage[0], sgb0=sgb[0], lg_sb=lg_sb,
                            mx_sb=mx_sb)
            fin = (("cDV", em.cnt["cDV"]), ("cAC", em.cnt["cAC"]),
                   ("cPE", em.cnt["cPE"]), ("cPL", em.cnt["cPL"]))
            for name in dump:
                t = all_bufs[name]
                shp = list(t.shape)
                dt = BF16 if t.dtype == BF16 else F32
                dten = nc.dram_tensor("dbg_" + name, shp, dt,
                                      kind="ExternalOutput").ap()
                sy(call("dma_start", dten, t[:, :]), inc="sO", waits=fin)

        sy(call("wait_ge", sems["sO"], em.cnt["sO"]))

        block = ctx.enter_context(nc.Block())
        eng_map = {"PE": block.tensor, "DV": block.vector, "AC": block.scalar,
                   "GP": block.gpsimd, "SY": block.sync}

        def replay(ename):
            def body(e):
                for ws, fn, inc, n in em.ops[ename]:
                    for s, v in ws:
                        e.wait_ge(sems[s], v)
                    ins = fn(e)
                    if inc is not None and ins is not None:
                        ins.then_inc(sems[inc], em.step[inc])
            return body

        for ename, dec in eng_map.items():
            dec(replay(ename))

    return nc



def _prep_shared(inputs):
    f32 = lambda a: np.ascontiguousarray(np.asarray(a, dtype=np.float32))
    qkv_w = f32(inputs["L0_attn_qkv_w"])
    proj_w = f32(inputs["L0_attn_proj_w"])
    w1 = f32(inputs["L0_ffn_w1"]); w2 = f32(inputs["L0_ffn_w2"])
    w3 = f32(inputs["L0_ffn_w3"])
    ws1 = f32(inputs["L0_ffn_ws1"]); ws2 = f32(inputs["L0_ffn_ws2"])
    ws3 = f32(inputs["L0_ffn_ws3"])
    gate_w = f32(inputs["L0_ffn_gate_w"])

    d = {}
    d["wqkvT"] = np.ascontiguousarray(
        qkv_w.reshape(48, 128, FC, 128).transpose(0, 2, 3, 1))
    d["wprojT"] = np.ascontiguousarray(
        proj_w.reshape(16, 128, FC, 128).transpose(0, 2, 3, 1))
    d["w1T"] = np.ascontiguousarray(
        w1.reshape(NE, HC, 128, FC, 128).transpose(0, 1, 3, 4, 2))
    d["w3T"] = np.ascontiguousarray(
        w3.reshape(NE, HC, 128, FC, 128).transpose(0, 1, 3, 4, 2))
    d["w2T"] = np.ascontiguousarray(
        w2.reshape(NE, 16, 128, HC, 128).transpose(1, 0, 3, 4, 2))
    d["ws1T"] = np.ascontiguousarray(
        ws1.reshape(HC, 128, FC, 128).transpose(0, 2, 3, 1))
    d["ws3T"] = np.ascontiguousarray(
        ws3.reshape(HC, 128, FC, 128).transpose(0, 2, 3, 1))
    d["ws2T"] = np.ascontiguousarray(
        ws2.reshape(16, 128, HC, 128).transpose(0, 2, 3, 1))
    d["gT"] = np.ascontiguousarray(gate_w.T.reshape(FC, 128, NE))
    d["nwq"] = np.ascontiguousarray(
        f32(inputs["L0_attn_qkv_norm_w"]).reshape(FC, 128).T)
    d["nwo"] = np.ascontiguousarray(
        f32(inputs["L0_attn_norm_w"]).reshape(FC, 128).T)
    d["nwp"] = np.ascontiguousarray(
        f32(inputs["L0_attn_proj_norm_w"]).reshape(FC, 128).T)
    d["srow"] = f32(inputs["L0_attn_s"]).reshape(1, H)

    p = np.arange(128, dtype=np.float32)
    cst = np.zeros((128, 512), dtype=np.float32)
    cst[:, 0:128] = p[None, :] - p[:, None]
    cst[:, 128:256] = 1e30 * (p[None, :] < p[:, None])
    cst[:, 256:384] = np.broadcast_to(p[None, :], (128, 128))
    cst[:, 384:400] = np.broadcast_to((128.0 - p)[:, None], (128, H))
    d["cst"] = cst
    sel8 = np.zeros((8, 1024), dtype=np.float32)
    for e in range(8):
        sel8[e, e * 128:(e + 1) * 128] = 1.0
    d["sel8"] = sel8.astype(BF)
    d["idnb"] = np.eye(128, dtype=np.float32).astype(BF)
    d["idnf"] = np.eye(128, dtype=np.float32)
    d["onesb"] = np.ones((128, 1), dtype=np.float32).astype(BF)
    d["onesf"] = np.ones((1, 128), dtype=np.float32)
    return d


def _prep_x(inputs):
    x = np.asarray(inputs["L0_attn_x"], dtype=np.float32)
    return [np.ascontiguousarray(x[SPC * c:SPC * (c + 1)].transpose(0, 2, 1))
            for c in range(NCORES)]


_BUILT = [None]


def kernel(**inputs):
    shared = _prep_shared(inputs)
    xs = _prep_x(inputs)
    if _BUILT[0] is None:
        _BUILT[0] = build_nc()
    nc = _BUILT[0]
    in_maps = [dict(shared, xT=xs[c]) for c in range(NCORES)]
    del shared, xs
    res = run_bass_kernel_spmd(nc, in_maps, core_ids=list(range(NCORES)))
    del in_maps
    out = np.empty((16, SEQ, D), dtype=np.float32)
    for c in range(NCORES):
        yTc = np.asarray(res.results[c]["yT"], dtype=np.float32)
        out[SPC * c:SPC * (c + 1)] = yTc.transpose(0, 2, 1)
    return out
